# version 03
# baseline (speedup 1.0000x reference)
import numpy as np
import concourse.bass as bass
import concourse.mybir as mybir
from concourse.bass_utils import run_bass_kernel_spmd
from contextlib import ExitStack

F32 = mybir.dt.float32
AF = mybir.ActivationFunctionType
ALU = mybir.AluOpType
AX = mybir.AxisListType.X

T = 2048
D = 1024
NF = 512
DEBUG = False
STOP_AT = None


class _DoneEmit(Exception):
    pass


class Prog:
    ENGS = ("t", "v", "s", "y", "g")
    EVENTS = ("t", "v", "s", "y", "g", "dy", "dg", "ds", "cc")

    def __init__(self):
        self.ops = {e: [] for e in self.ENGS}
        self.count = {e: 0 for e in self.EVENTS}
        self.hwm = {e: {} for e in self.ENGS}
        self.fence_snap = []
        self.sems = {}

    def op(self, eng, fn, after=(), dma=False, cc=False):
        waits = list(self.fence_snap)
        for a in after:
            if a is not None:
                waits.append(a)
        if cc:
            ev, amt = "cc", 1
        elif dma:
            ev, amt = "d" + eng, 16
        else:
            ev, amt = eng, 1
        self.count[ev] += amt
        tok = (ev, self.count[ev])
        self.ops[eng].append((fn, waits, (ev, amt)))
        return tok

    def fence(self):
        self.fence_snap = [(ev, c) for ev, c in self.count.items() if c > 0]

    def replay(self, eng, engine):
        h = self.hwm[eng]
        n_own = 0
        for fn, waits, (ev, amt) in self.ops[eng]:
            if eng in ("v", "s") and n_own > 0:
                if h.get(eng, 0) < n_own:
                    engine.wait_ge(self.sems[eng], n_own)
                    h[eng] = n_own
            for wev, val in waits:
                if h.get(wev, 0) < val:
                    engine.wait_ge(self.sems[wev], val)
                    h[wev] = val
            fn(engine).then_inc(self.sems[ev], amt)
            if ev == eng:
                n_own = max(n_own, self.count_at(eng, fn) if False else 0)
            n_own += amt if ev == eng else 0

    def final_waits(self, eng, engine):
        h = self.hwm[eng]
        for ev, c in self.count.items():
            if c > 0 and h.get(ev, 0) < c:
                engine.wait_ge(self.sems[ev], c)
                h[ev] = c


def host_constants():
    d = np.arange(D, dtype=np.float64)
    k = np.arange(NF, dtype=np.float64)
    ang = 2.0 * np.pi * np.outer(d, k) / D
    C = np.cos(ang)
    S = -np.sin(ang)
    S[:, 0] = (-1.0) ** d
    kd = 2.0 * np.pi * np.outer(k, d) / D
    C2 = 2.0 * np.cos(kd) / D
    C2[0, :] = 1.0 / D
    S2 = -2.0 * np.sin(kd) / D
    S2[0, :] = ((-1.0) ** d) / D
    causal = np.where(np.arange(128)[:, None] >= np.arange(128)[None, :], 0.0, -1e10)
    return {
        "cst_C": np.ascontiguousarray(C.astype(np.float32)),
        "cst_S": np.ascontiguousarray(S.astype(np.float32)),
        "cst_C2": np.ascontiguousarray(C2.astype(np.float32)),
        "cst_S2": np.ascontiguousarray(S2.astype(np.float32)),
        "cst_ident": np.eye(128, dtype=np.float32),
        "cst_causal": np.ascontiguousarray(causal.astype(np.float32)),
        "cst_idxm8": (np.arange(8, dtype=np.float32) - 8.0).reshape(1, 8),
        "cst_idx8": np.arange(8, dtype=np.float32).reshape(1, 8),
    }


def build_program(stop_at=None):
    nc = bass.Bass("TRN2", target_bir_lowering=False)

    def din(name, shape):
        return nc.dram_tensor(name, shape, F32, kind="ExternalInput").ap()

    x_in = din("x", [T, D])
    Wq, Wk, Wv = din("Wq", [D, D]), din("Wk", [D, D]), din("Wv", [D, D])
    bq, bk, bv = din("bq", [1, D]), din("bk", [1, D]), din("bv", [1, D])
    gamma, beta = din("gamma", [1, D]), din("beta", [1, D])
    mu_in = din("mu", [D, 256]); f0scale = din("f0scale", [1, 1])
    w1_in, b1_in = din("w1", [16, D, 256]), din("b1", [16, 256])
    w2_in, b2_in = din("w2", [16, 256, D]), din("b2", [16, D])
    wqkv, wout = din("wqkv", [D, 3 * D]), din("wout", [D, D])
    gateW = din("gateW", [8, D])
    Wgate, Wup = din("Wgate", [8, 4096, D]), din("Wup", [8, 4096, D])
    Wdown = din("Wdown", [8, D, 4096])
    coreoh = din("coreoh", [1, 8])
    Cm, Sm = din("cst_C", [D, NF]), din("cst_S", [D, NF])
    C2m, S2m = din("cst_C2", [NF, D]), din("cst_S2", [NF, D])
    identD, causalD = din("cst_ident", [128, 128]), din("cst_causal", [128, 128])
    idxm8D, idx8D = din("cst_idxm8", [1, 8]), din("cst_idx8", [1, 8])

    out_d = nc.dram_tensor("out", [T, D], F32, kind="ExternalOutput").ap()

    scr = lambda name, shape, **kw: nc.dram_tensor(name, shape, F32, **kw).ap()
    d_xT = scr("d_xT", [D, T])
    d_xnT = scr("d_xnT", [D, T])
    d_xntok = scr("d_xntok", [T, D])
    d_G = scr("d_G", [6, D, NF])
    d_a0T = scr("d_a0T", [D, T])
    d_xn2T = scr("d_xn2T", [D, T])
    d_xn2tok = scr("d_xn2tok", [T, D])
    d_rl = scr("d_rl", [1, T])
    d_x1T = scr("d_x1T", [D, T])
    d_stat1 = scr("d_stat1", [2, T])
    d_qkT = scr("d_qkT", [2 * D, T])
    d_vtok = scr("d_vtok", [T, D])
    d_at1T = scr("d_at1T", [D, T])
    d_wtok = scr("d_wtok", [128, 128])
    d_rep = scr("d_rep", [4, T])
    d_ag_in = scr("d_ag_in", [8, T])
    d_ag_out = scr("d_ag_out", [64, T], addr_space="Shared")

    dbg = {}
    if DEBUG:
        for nm, shp in [("dbg_V0", [128, 64]), ("dbg_xntok", [T, D]), ("dbg_xnT", [D, T]), ("dbg_a0T", [D, T]), ("dbg_x1T", [D, T]),
                        ("dbg_at1T", [D, T]), ("dbg_pi", [128, 1024]), ("dbg_wtok", [128, 128]),
                        ("dbg_xn2T", [D, T])]:
            dbg[nm] = nc.dram_tensor(nm, shp, F32, kind="ExternalOutput").ap()

    es = ExitStack()
    sb = lambda nm, cols: es.enter_context(nc.sbuf_tensor(nm, [128, cols], F32))
    A0 = sb("A0", 16384)
    A1 = sb("A1", 16384)
    A2 = sb("A2", 4096)
    A3 = sb("A3", 4096)
    A4 = sb("A4", 4096)
    A5 = sb("A5", 4096)
    A6 = sb("A6", 2624)
    V0 = sb("V0", 512)
    CS_ = sb("CST", 896)
    PS = es.enter_context(nc.psum_tensor("PS", [128, 2048], F32))
    PB = es.enter_context(nc.psum_tensor("PB", [128, 2048], F32))

    IDENT = CS_[:, 0:128]
    CAUSAL = CS_[:, 128:256]
    ONES512 = CS_[:, 256:768]
    BT = CS_[:, 768:792]
    IDXM8 = CS_[:, 800:808]
    IDX8 = CS_[:, 808:816]
    CARRY = CS_[:, 816:824]
    ONESCOL = CS_[:, 824:825]
    COH = CS_[:, 832:840]

    def bank(i):
        t_ = PS if i < 4 else PB
        return t_[:, (i % 4) * 512:(i % 4 + 1) * 512]

    p = Prog()

    def dma(eng, out, in_, after=()):
        return p.op(eng, lambda e, o=out, i=in_: e.dma_start(out=o, in_=i), after=after, dma=True)

    def dma_slow(eng, out, in_, after=()):
        return p.op(eng, lambda e, o=out, i=in_: e.dma_start(
            out=o, in_=i, allow_slow_non_contiguous=True), after=after, dma=True)


    def cut(name):
        if stop_at == name:
            if DEBUG:
                p.fence()
                dma("g", dbg["dbg_V0"], V0[:, 0:64])
                p.fence()
            raise _DoneEmit()

    def ld(eng, sbuf_slice, dram_ap, inner, after=()):
        src3 = dram_ap.rearrange("(n p) d -> p n d", p=128)
        dst3 = sbuf_slice.rearrange("p (n d) -> p n d", d=inner)
        return dma(eng, dst3, src3, after=after)

    def st(eng, dram_ap, sbuf_slice, inner, after=()):
        dst3 = dram_ap.rearrange("(n p) d -> p n d", p=128)
        src3 = sbuf_slice.rearrange("p (n d) -> p n d", d=inner)
        return dma(eng, dst3, src3, after=after)

    vop = lambda fn, after=(): p.op("v", fn, after=after)
    sop = lambda fn, after=(): p.op("s", fn, after=after)
    top = lambda fn, after=(): p.op("t", fn, after=after)

    tr_state = {"i": 0, "guard": [None, None]}

    def tr(src, dst, after=(), evac_eng="v"):
        slot = tr_state["i"] % 2
        tr_state["i"] += 1
        p0, f0 = src.shape[0], src.shape[1]
        sl = PB[:, 1024 + slot * 512: 1024 + slot * 512 + 128]
        deps = list(after) + [tr_state["guard"][slot]]
        tk = top(lambda e, s=src, sl=sl, p0=p0, f0=f0: e.transpose(
            sl[0:f0, 0:p0], s, IDENT[0:p0, 0:p0]), after=deps)
        if evac_eng == "v":
            ev = vop(lambda e, d=dst, sl=sl, p0=p0, f0=f0: e.tensor_copy(
                out=d, in_=sl[0:f0, 0:p0]), after=[tk])
        else:
            ev = sop(lambda e, d=dst, sl=sl, p0=p0, f0=f0: e.activation(
                out=d, in_=sl[0:f0, 0:p0], func=AF.Copy), after=[tk])
        tr_state["guard"][slot] = ev
        return ev

    try:
        dma("y", IDENT, identD)
        dma("y", CAUSAL, causalD)
        dma("y", IDXM8, idxm8D.partition_broadcast(128))
        dma("y", IDX8, idx8D.partition_broadcast(128))
        dma("y", COH, coreoh.partition_broadcast(128))
        vop(lambda e: e.memset(ONES512, 1.0))
        vop(lambda e: e.memset(ONESCOL, 1.0))
        vop(lambda e: e.memset(CARRY, 0.0))

        cut("C0")
        ld("y", A0[:, :], x_in, D)
        p.fence()
        for t in range(16):
            xt = A0[:, t * D:(t + 1) * D]
            vop(lambda e, xt=xt, t=t: e.reduce_sum(V0[:, t:t + 1], xt, axis=AX))
            vop(lambda e, xt=xt, t=t: e.tensor_mul(A1[:, t * D:(t + 1) * D], xt, xt))
        p.fence()
        for t in range(16):
            vop(lambda e, t=t: e.reduce_sum(V0[:, 16 + t:17 + t], A1[:, t * D:(t + 1) * D], axis=AX))
        p.fence()
        cut("INsums")
        vop(lambda e: e.tensor_scalar_mul(V0[:, 0:16], V0[:, 0:16], 1.0 / D))
        vop(lambda e: e.tensor_scalar_mul(V0[:, 16:32], V0[:, 16:32], 1.0 / D))
        vop(lambda e: e.tensor_mul(V0[:, 32:48], V0[:, 0:16], V0[:, 0:16]))
        vop(lambda e: e.tensor_sub(V0[:, 16:32], V0[:, 16:32], V0[:, 32:48]))
        p.fence()
        vop(lambda e: e.tensor_scalar_add(V0[:, 16:32], V0[:, 16:32], 1e-5))
        p.fence()
        sop(lambda e: e.activation(out=V0[:, 32:48], in_=V0[:, 16:32], func=AF.Sqrt))
        p.fence()
        vop(lambda e: e.reciprocal(V0[:, 48:64], V0[:, 32:48]))
        p.fence()
        for t in range(16):
            vop(lambda e, t=t: e.tensor_scalar(
                out=A1[:, t * D:(t + 1) * D], in0=A0[:, t * D:(t + 1) * D],
                scalar1=V0[:, t:t + 1], scalar2=V0[:, 48 + t:49 + t],
                op0=ALU.subtract, op1=ALU.mult))
        p.fence()
        st("g", d_xntok, A1[:, :], D)
        if DEBUG:
            p.fence()
            dma("g", dbg["dbg_xntok"], d_xntok)
            dma("g", dbg["dbg_V0"], V0[:, 0:64])
            p.fence()
        cut("IN1")
        for (src, dst) in ((A1, d_xnT), (A0, d_xT)):
            for c in range(8):
                toks = [tr(src[:, t * D + c * 128: t * D + (c + 1) * 128],
                           A2[:, t * 128:(t + 1) * 128]) for t in range(16)]
                dma("g", dst[c * 128:(c + 1) * 128, :], A2[:, 0:2048], after=toks)
                p.fence()
        if DEBUG:
            dma("g", dbg["dbg_xnT"], d_xnT)
            dma("g", dbg["dbg_xntok"], d_xntok)

        cut("IN")
        ld("y", A2[:, :], Cm, 512)
        ld("y", A3[:, :], Sm, 512)
        p.fence()
        for sig, (Wm, bm) in enumerate(((Wq, bq), (Wk, bk), (Wv, bv))):
            dma_slow("y", V0[:, 0:8], bm.rearrange("o (c p) -> p (o c)", p=128))
            p.fence()
            for comp, CSbuf in enumerate((A2, A3)):
                sc = sig * 2 + comp
                last = {}
                for ec in range(8):
                    half = A5[:, (ec % 2) * D:(ec % 2) * D + D]
                    lt = dma("y", half, Wm[ec * 128:(ec + 1) * 128, :])
                    for dc in range(8):
                        last[dc] = top(lambda e, half=half, dc=dc, ec=ec, CSbuf=CSbuf: e.matmul(
                            bank(dc), half[:, dc * 128:(dc + 1) * 128],
                            CSbuf[:, ec * 512:(ec + 1) * 512],
                            start=(ec == 0), stop=(ec == 7)), after=[lt])
                    p.fence()
                ev = [vop(lambda e, dc=dc: e.tensor_copy(out=A4[:, dc * 512:(dc + 1) * 512],
                                                         in_=bank(dc)), after=[last[dc]])
                      for dc in range(8)]
                st("g", d_G[sc], A4[:, 0:4096], 512, after=ev)
                p.fence()
                blast = None
                for ec in range(8):
                    blast = top(lambda e, ec=ec, CSbuf=CSbuf: e.matmul(
                        bank(0)[0:1, :], V0[:, ec:ec + 1], CSbuf[:, ec * 512:(ec + 1) * 512],
                        start=(ec == 0), stop=(ec == 7)))
                bev = vop(lambda e: e.tensor_copy(out=A6[0:1, 0:512], in_=bank(0)[0:1, :]), after=[blast])
                dmab = dma("g", d_rep[2:3, 0:512], A6[0:1, 0:512], after=[bev])
                dma_slow("y", BT[:, sc * 4:(sc + 1) * 4],
                    d_rep[2:3, 0:512].rearrange("o (c p) -> p (o c)", p=128), after=[dmab])
                p.fence()

        cut("G")
        def spv(sc, fc):
            return A0[:, sc * 2048 + fc * 512: sc * 2048 + (fc + 1) * 512]
        TMP1 = A0[:, 12288:12800]; TMP2 = A0[:, 12800:13312]
        RK = A0[:, 13312:13824]; RV = A0[:, 13824:14336]; RQ = A0[:, 14336:14848]
        BCB = A0[:, 14848:15360]; BSB = A0[:, 15360:15872]
        R0 = A1[0:1, :]

        for tt in range(4):
            ltx = ld("y", A3[:, :], d_xnT[:, tt * 512:(tt + 1) * 512], 512)
            for sc in range(6):
                lg = ld("y", A2[:, :], d_G[sc], 512)
                mm_last = {}
                for fc in range(4):
                    for dc in range(8):
                        mm_last[fc] = top(lambda e, fc=fc, dc=dc: e.matmul(
                            bank(fc), A2[:, dc * 512 + fc * 128: dc * 512 + (fc + 1) * 128],
                            A3[:, dc * 512:(dc + 1) * 512],
                            start=(dc == 0), stop=(dc == 7)), after=[lg, ltx])
                for fc in range(4):
                    vop(lambda e, sc=sc, fc=fc: e.tensor_scalar(
                        out=spv(sc, fc), in0=bank(fc), scalar1=BT[:, sc * 4 + fc: sc * 4 + fc + 1],
                        scalar2=None, op0=ALU.add), after=[mm_last[fc]])
                p.fence()
            for fc in range(4):
                QC, QS = spv(0, fc), spv(1, fc)
                KC, KS = spv(2, fc), spv(3, fc)
                VC, VS = spv(4, fc), spv(5, fc)
                if fc == 0:
                    r0q_c, r0q_s = R0[:, 0:512], R0[:, 512:1024]
                    r0b_c, r0b_s = R0[:, 1024:1536], R0[:, 1536:2048]
                    t1, t2 = R0[:, 2048:2560], R0[:, 2560:3072]
                    for (Xr, dstn) in ((QC, r0q_c), (QS, r0q_s)):
                        atk = sop(lambda e, Xr=Xr, t1=t1: e.activation(out=t1, in_=Xr[0:1, :], func=AF.Abs))
                        atk = vop(lambda e, t1=t1: e.tensor_scalar_add(t1, t1, 1e-8), after=[atk])
                        atk = vop(lambda e, t2=t2, t1=t1: e.reciprocal(t2, t1), after=[atk])
                        vop(lambda e, Xr=Xr, dstn=dstn, t2=t2: e.tensor_mul(dstn, Xr[0:1, :], t2), after=[atk])
                        p.fence()
                    for (Kr, Vr, dstn) in ((KC, VC, r0b_c), (KS, VS, r0b_s)):
                        a1t = sop(lambda e, Kr=Kr, t1=t1: e.activation(out=t1, in_=Kr[0:1, :], func=AF.Abs))
                        a1t = vop(lambda e, t1=t1: e.tensor_scalar_add(t1, t1, 1e-8), after=[a1t])
                        a2t = sop(lambda e, Vr=Vr, t2=t2: e.activation(out=t2, in_=Vr[0:1, :], func=AF.Abs))
                        a2t = vop(lambda e, t2=t2: e.tensor_scalar_add(t2, t2, 1e-8), after=[a2t])
                        p.fence()
                        vop(lambda e, t1=t1, t2=t2: e.tensor_mul(t1, t1, t2))
                        p.fence()
                        vop(lambda e, t2=t2, t1=t1: e.reciprocal(t2, t1))
                        vop(lambda e, Kr=Kr, Vr=Vr, t1=t1: e.tensor_mul(t1, Kr[0:1, :], Vr[0:1, :]))
                        p.fence()
                        vop(lambda e, dstn=dstn, t1=t1, t2=t2: e.tensor_mul(dstn, t1, t2))
                        p.fence()
                for (Xc, Xs, R) in ((QC, QS, RQ), (KC, KS, RK), (VC, VS, RV)):
                    vop(lambda e, Xc=Xc: e.tensor_mul(TMP1, Xc, Xc))
                    vop(lambda e, Xs=Xs: e.tensor_mul(TMP2, Xs, Xs))
                    p.fence()
                    vop(lambda e: e.tensor_add(TMP1, TMP1, TMP2))
                    p.fence()
                    sop(lambda e: e.activation(out=TMP2, in_=TMP1, func=AF.Sqrt))
                    p.fence()
                    vop(lambda e: e.tensor_scalar_add(TMP2, TMP2, 1e-8))
                    p.fence()
                    vop(lambda e, R=R: e.reciprocal(R, TMP2))
                    p.fence()
                vop(lambda e, QC=QC: e.tensor_mul(QC, QC, RQ))
                vop(lambda e, QS=QS: e.tensor_mul(QS, QS, RQ))
                vop(lambda e: e.tensor_mul(RK, RK, RV))
                p.fence()
                vop(lambda e, KC=KC, VC=VC: e.tensor_mul(TMP1, KC, VC))
                vop(lambda e, KS=KS, VS=VS: e.tensor_mul(TMP2, KS, VS))
                p.fence()
                vop(lambda e: e.tensor_sub(BCB, TMP1, TMP2))
                vop(lambda e, KC=KC, VS=VS: e.tensor_mul(TMP1, KC, VS))
                vop(lambda e, KS=KS, VC=VC: e.tensor_mul(TMP2, KS, VC))
                p.fence()
                vop(lambda e: e.tensor_add(BSB, TMP1, TMP2))
                p.fence()
                vop(lambda e: e.tensor_mul(BCB, BCB, RK))
                vop(lambda e: e.tensor_mul(BSB, BSB, RK))
                p.fence()
                if fc == 0:
                    vop(lambda e, QC=QC: e.tensor_copy(out=QC[0:1, :], in_=R0[:, 0:512]))
                    vop(lambda e, QS=QS: e.tensor_copy(out=QS[0:1, :], in_=R0[:, 512:1024]))
                    vop(lambda e: e.tensor_copy(out=BCB[0:1, :], in_=R0[:, 1024:1536]))
                    vop(lambda e: e.tensor_copy(out=BSB[0:1, :], in_=R0[:, 1536:2048]))
                    p.fence()
                vop(lambda e, KC=KC, fc=fc: e.tensor_tensor_scan(
                    out=KC, data0=ONES512, data1=BCB, initial=CARRY[:, fc * 2:fc * 2 + 1],
                    op0=ALU.mult, op1=ALU.add))
                vop(lambda e, KS=KS, fc=fc: e.tensor_tensor_scan(
                    out=KS, data0=ONES512, data1=BSB, initial=CARRY[:, fc * 2 + 1:fc * 2 + 2],
                    op0=ALU.mult, op1=ALU.add))
                p.fence()
                vop(lambda e, KC=KC, fc=fc: e.tensor_copy(out=CARRY[:, fc * 2:fc * 2 + 1], in_=KC[:, 511:512]))
                vop(lambda e, KS=KS, fc=fc: e.tensor_copy(out=CARRY[:, fc * 2 + 1:fc * 2 + 2], in_=KS[:, 511:512]))
                vop(lambda e, KC=KC, QC=QC: e.tensor_mul(TMP1, KC, QC))
                vop(lambda e, KS=KS, QS=QS: e.tensor_mul(TMP2, KS, QS))
                p.fence()
                vop(lambda e, VC=VC: e.tensor_add(VC, TMP1, TMP2))
                vop(lambda e, KS=KS, QC=QC: e.tensor_mul(TMP1, KS, QC))
                vop(lambda e, KC=KC, QS=QS: e.tensor_mul(TMP2, KC, QS))
                p.fence()
                vop(lambda e, VS=VS: e.tensor_sub(VS, TMP1, TMP2))
                p.fence()
                if fc == 0:
                    vop(lambda e, VC=VC, KC=KC, QC=QC: e.tensor_mul(VC[0:1, :], KC[0:1, :], QC[0:1, :]))
                    vop(lambda e, VS=VS, KS=KS, QS=QS: e.tensor_mul(VS[0:1, :], KS[0:1, :], QS[0:1, :]))
                    p.fence()
            lc2 = ld("y", A2[:, :], C2m, D)
            ls2 = ld("y", A3[:, :], S2m, D)
            ev_toks = []
            for dc in range(8):
                last = None
                for fc in range(4):
                    last = top(lambda e, dc=dc, fc=fc: e.matmul(
                        bank(dc), A2[:, fc * D + dc * 128: fc * D + (dc + 1) * 128], spv(4, fc),
                        start=(fc == 0), stop=False), after=[lc2, ls2])
                for fc in range(4):
                    last = top(lambda e, dc=dc, fc=fc: e.matmul(
                        bank(dc), A3[:, fc * D + dc * 128: fc * D + (dc + 1) * 128], spv(5, fc),
                        start=False, stop=(fc == 3)))
                ev_toks.append(vop(lambda e, dc=dc: e.tensor_copy(
                    out=A4[:, dc * 512:(dc + 1) * 512], in_=bank(dc)), after=[last]))
            st("g", d_a0T[:, tt * 512:(tt + 1) * 512], A4[:, 0:4096], 512, after=ev_toks)
            p.fence()
        if DEBUG:
            dma("g", dbg["dbg_a0T"], d_a0T)

        cut("A0")
        ld("y", A0[:, :], d_xntok, D)
        dma("y", A2[:, 0:D], gamma.partition_broadcast(128))
        dma("y", A2[:, D:2 * D], beta.partition_broadcast(128))
        p.fence()
        for t in range(16):
            xt = A0[:, t * D:(t + 1) * D]
            half = A5[:, 0:D] if t % 2 == 0 else A5[:, D:2 * D]
            stok = sop(lambda e, xt=xt, half=half: e.activation(out=half, in_=xt, func=AF.Square))
            vop(lambda e, t=t, half=half: e.reduce_sum(V0[:, t:t + 1], half, axis=AX), after=[stok])
            p.fence()
        vop(lambda e: e.tensor_scalar_mul(V0[:, 0:16], V0[:, 0:16], 1.0 / D))
        p.fence()
        vop(lambda e: e.tensor_scalar_add(V0[:, 0:16], V0[:, 0:16], 1e-5))
        p.fence()
        sop(lambda e: e.activation(out=V0[:, 16:32], in_=V0[:, 0:16], func=AF.Sqrt))
        p.fence()
        vop(lambda e: e.reciprocal(V0[:, 32:48], V0[:, 16:32]))
        p.fence()
        for t in range(16):
            xt = A0[:, t * D:(t + 1) * D]
            vop(lambda e, xt=xt, t=t: e.tensor_scalar_mul(xt, xt, V0[:, 32 + t:33 + t]))
        p.fence()
        for t in range(16):
            xt = A0[:, t * D:(t + 1) * D]
            vop(lambda e, xt=xt: e.tensor_mul(xt, xt, A2[:, 0:D]))
        p.fence()
        for t in range(16):
            xt = A0[:, t * D:(t + 1) * D]
            vop(lambda e, xt=xt: e.tensor_add(xt, xt, A2[:, D:2 * D]))
        p.fence()
        st("g", d_xn2tok, A0[:, :], D)
        for t in range(16):
            xt = A0[:, t * D:(t + 1) * D]
            half = A5[:, 0:D] if t % 2 == 0 else A5[:, D:2 * D]
            stok = sop(lambda e, xt=xt, half=half: e.activation(out=half, in_=xt, func=AF.Square))
            vop(lambda e, t=t, half=half: e.reduce_sum(V0[:, 64 + t:65 + t], half, axis=AX), after=[stok])
            p.fence()
        sop(lambda e: e.activation(out=V0[:, 80:96], in_=V0[:, 64:80], func=AF.Sqrt))
        p.fence()
        vop(lambda e: e.tensor_scalar_max(V0[:, 80:96], V0[:, 80:96], 1e-12))
        p.fence()
        vop(lambda e: e.reciprocal(V0[:, 96:112], V0[:, 80:96]))
        p.fence()
        dma("y", V0[:, 129:130], f0scale.partition_broadcast(128))
        p.fence()
        vop(lambda e: e.tensor_scalar(out=V0[:, 96:112], in0=V0[:, 96:112],
                                      scalar1=V0[:, 129:130], scalar2=None, op0=ALU.mult))
        p.fence()
        dma_slow("g", d_rl.rearrange("o (n p) -> p (o n)", p=128), V0[:, 96:112])
        p.fence()
        dma("y", A6[:, 0:2048], d_rl.partition_broadcast(128))
        for c in range(8):
            toks = [tr(A0[:, t * D + c * 128: t * D + (c + 1) * 128],
                       A5[:, t * 128:(t + 1) * 128]) for t in range(16)]
            dma("g", d_xn2T[c * 128:(c + 1) * 128, :], A5[:, 0:2048], after=toks)
            p.fence()
        if DEBUG:
            dma("g", dbg["dbg_xn2T"], d_xn2T)
        cut("F0a")
        ld("y", A3[:, 0:2048], mu_in, 256)
        p.fence()
        for c in range(8):
            sop(lambda e, c=c: e.activation(out=A5[:, c * 256:(c + 1) * 256],
                                            in_=A3[:, c * 256:(c + 1) * 256], func=AF.Square))
        p.fence()
        last = None
        for c in range(8):
            last = top(lambda e, c=c: e.matmul(bank(0)[0:1, 0:256], ONESCOL,
                                               A5[:, c * 256:(c + 1) * 256],
                                               start=(c == 0), stop=(c == 7)))
        p.fence()
        sop(lambda e: e.activation(out=A5[0:1, 0:256], in_=bank(0)[0:1, 0:256], func=AF.Sqrt))
        p.fence()
        vop(lambda e: e.tensor_scalar_max(A5[0:1, 0:256], A5[0:1, 0:256], 1e-12))
        p.fence()
        vop(lambda e: e.reciprocal(A5[0:1, 0:256], A5[0:1, 0:256]))
        p.fence()
        dma("g", d_rep[3:4, 0:256], A5[0:1, 0:256])
        p.fence()
        dma("y", A3[:, 2048:2304], d_rep[3:4, 0:256].partition_broadcast(128))
        p.fence()
        for c in range(8):
            vop(lambda e, c=c: e.tensor_mul(A3[:, c * 256:(c + 1) * 256],
                                            A3[:, c * 256:(c + 1) * 256], A3[:, 2048:2304]))
        p.fence()
        cut("F0b")
        for tt in range(4):
            ltk = ld("y", A2[:, :], d_xn2T[:, tt * 512:(tt + 1) * 512], 512)
            for ec in range(2):
                last = None
                for dc in range(8):
                    last = top(lambda e, ec=ec, dc=dc: e.matmul(
                        bank(ec), A3[:, dc * 256 + ec * 128: dc * 256 + (ec + 1) * 128],
                        A2[:, dc * 512:(dc + 1) * 512],
                        start=(dc == 0), stop=(dc == 7)), after=[ltk])
                vop(lambda e, ec=ec, tt=tt: e.tensor_mul(
                    A1[:, ec * 2048 + tt * 512: ec * 2048 + (tt + 1) * 512],
                    bank(ec), A6[:, tt * 512:(tt + 1) * 512]), after=[last])
            p.fence()
        for ec in range(2):
            lt = A1[:, ec * 2048:(ec + 1) * 2048]
            dt_ = A1[:, 4096 + ec * 2048: 4096 + (ec + 1) * 2048]
            vop(lambda e, lt=lt: e.reduce_max(V0[:, 200:201], lt, axis=AX))
            p.fence()
            vop(lambda e: e.tensor_scalar_mul(V0[:, 201:202], V0[:, 200:201], -1.0))
            p.fence()
            sop(lambda e, lt=lt, dt_=dt_: e.activation(out=dt_, in_=lt, func=AF.Exp,
                                                       bias=V0[:, 201:202], accum_out=V0[:, 202:203]))
            p.fence()
            vop(lambda e: e.reciprocal(V0[:, 203:204], V0[:, 202:203]))
            p.fence()
            vop(lambda e, dt_=dt_: e.tensor_scalar(out=dt_, in0=dt_, scalar1=V0[:, 203:204],
                                                   scalar2=None, op0=ALU.mult))
            p.fence()
        for t in range(16):
            for c in range(2):
                tr(A1[:, 4096 + c * 2048 + t * 128: 4096 + c * 2048 + (t + 1) * 128],
                   A2[:, t * 256 + c * 128: t * 256 + (c + 1) * 128])
        p.fence()
        for t in range(16):
            for c in range(2):
                tr(A1[:, c * 2048 + t * 128: c * 2048 + (t + 1) * 128],
                   A1[:, 8192 + t * 256 + c * 128: 8192 + t * 256 + (c + 1) * 128])
        p.fence()
        for t in range(16):
            lt = A1[:, 8192 + t * 256: 8192 + (t + 1) * 256]
            vop(lambda e, lt=lt: e.reduce_max(V0[:, 210:211], lt, axis=AX))
            p.fence()
            vop(lambda e: e.tensor_scalar_mul(V0[:, 211:212], V0[:, 210:211], -1.0))
            p.fence()
            sop(lambda e, lt=lt: e.activation(out=lt, in_=lt, func=AF.Exp,
                                              bias=V0[:, 211:212], accum_out=V0[:, 212:213]))
            p.fence()
            vop(lambda e: e.reciprocal(V0[:, 213:214], V0[:, 212:213]))
            p.fence()
            vop(lambda e, lt=lt: e.tensor_scalar(out=lt, in0=lt, scalar1=V0[:, 213:214],
                                                 scalar2=None, op0=ALU.mult))
            p.fence()
        for t in range(16):
            for c in range(2):
                tr(A1[:, 8192 + t * 256 + c * 128: 8192 + t * 256 + (c + 1) * 128],
                   A1[:, 12288 + c * 2048 + t * 128: 12288 + c * 2048 + (t + 1) * 128])
        p.fence()
        cut("F0c")
        ld("y", A0[:, :], d_xn2tok, D)
        p.fence()
        for dc in range(8):
            last = None
            for t in range(16):
                last = top(lambda e, dc=dc, t=t: e.matmul(
                    bank(dc)[:, 0:256], A0[:, t * D + dc * 128: t * D + (dc + 1) * 128],
                    A2[:, t * 256:(t + 1) * 256],
                    start=(t == 0), stop=(t == 15)))
            vop(lambda e, dc=dc: e.tensor_copy(out=A4[:, dc * 256:(dc + 1) * 256],
                                               in_=bank(dc)[:, 0:256]), after=[last])
        p.fence()
        cut("F0d")
        for ex in range(16):
            lw1 = ld("y", A5[:, 0:2048], w1_in[ex], 256)
            lb1 = dma("y", A6[0:16, 0:256], b1_in[ex:ex + 1, :].partition_broadcast(16))
            last = None
            for dc in range(8):
                last = top(lambda e, ex=ex, dc=dc: e.matmul(
                    bank(0)[0:16, 0:256], A4[:, dc * 256 + ex * 16: dc * 256 + ex * 16 + 16],
                    A5[:, dc * 256:(dc + 1) * 256],
                    start=(dc == 0), stop=(dc == 7)), after=[lw1])
            hb = A6[0:16, 256:512]
            tk = vop(lambda e, hb=hb: e.tensor_add(hb, bank(0)[0:16, 0:256], A6[0:16, 0:256]),
                     after=[last, lb1])
            tk = sop(lambda e, hb=hb: e.activation(out=hb, in_=hb, func=AF.Gelu), after=[tk])
            cut("F0d1")
            trt = [tr(hb[:, c * 128:(c + 1) * 128], A6[:, 512 + c * 16: 512 + (c + 1) * 16],
                      after=[tk]) for c in range(2)]
            cut("F0d2")
            lw2 = ld("y", A5[:, 2048:4096], w2_in[ex], D)
            lb2 = dma("y", A6[0:16, 544:1568], b2_in[ex:ex + 1, :].partition_broadcast(16))
            p.fence()
            cut("F0d2b")
            so_toks = []
            prev_ev = None
            for db in range(2):
                last = None
                for hc in range(2):
                    last = top(lambda e, db=db, hc=hc: e.matmul(
                        bank(1)[0:16, :], A6[:, 512 + hc * 16: 512 + (hc + 1) * 16],
                        A5[:, 2048 + hc * 1024 + db * 512: 2048 + hc * 1024 + (db + 1) * 512],
                        start=(hc == 0), stop=(hc == 1)), after=[lw2, prev_ev] + trt)
                if db == 0:
                    cut("F0d2c")
                prev_ev = vop(lambda e, db=db: e.tensor_add(
                    A6[0:16, 1568 + db * 512: 1568 + (db + 1) * 512], bank(1)[0:16, :],
                    A6[0:16, 544 + db * 512: 544 + (db + 1) * 512]), after=[last, lb2])
                so_toks.append(prev_ev)
                if db == 0:
                    cut("F0d2d")
            cut("F0d3")
            dma("g", A4[16 * (ex % 8):16 * (ex % 8) + 16,
                        2048 + (ex // 8) * 1024: 2048 + (ex // 8 + 1) * 1024],
                A6[0:16, 1568:2592], after=so_toks)
            p.fence()
            cut("F0d4")
        cut("F0e")
        for tt in range(4):
            lx = ld("y", A2[:, :], d_xT[:, tt * 512:(tt + 1) * 512], 512)
            la = ld("y", A3[:, :], d_a0T[:, tt * 512:(tt + 1) * 512], 512)
            t2_toks = []
            for dc in range(8):
                last = None
                for ec in range(2):
                    last = top(lambda e, dc=dc, ec=ec, tt=tt: e.matmul(
                        bank(dc), A4[:, 2048 + ec * 1024 + dc * 128: 2048 + ec * 1024 + (dc + 1) * 128],
                        A1[:, 12288 + ec * 2048 + tt * 512: 12288 + ec * 2048 + (tt + 1) * 512],
                        start=(ec == 0), stop=(ec == 1)))
                t1tok = vop(lambda e, dc=dc: e.tensor_add(A5[:, dc * 512:(dc + 1) * 512],
                                                          bank(dc), A2[:, dc * 512:(dc + 1) * 512]),
                            after=[last, lx])
                t2_toks.append(vop(lambda e, dc=dc: e.tensor_add(A5[:, dc * 512:(dc + 1) * 512],
                                                                 A5[:, dc * 512:(dc + 1) * 512],
                                                                 A3[:, dc * 512:(dc + 1) * 512]),
                                   after=[t1tok, la]))
            st("g", d_x1T[:, tt * 512:(tt + 1) * 512], A5[:, 0:4096], 512, after=t2_toks)
            p.fence()
        if DEBUG:
            dma("g", dbg["dbg_x1T"], d_x1T)

        cut("F0")
        ld("y", A0[:, :], d_x1T, T)
        p.fence()
        for c in range(8):
            half = A5[:, 0:2048] if c % 2 == 0 else A5[:, 2048:4096]
            stok = sop(lambda e, c=c, half=half: e.activation(out=half, in_=A0[:, c * 2048:(c + 1) * 2048],
                                                              func=AF.Square))
            for tb in range(4):
                top(lambda e, c=c, tb=tb: e.matmul(
                    bank(tb)[0:1, :], ONESCOL, A0[:, c * 2048 + tb * 512: c * 2048 + (tb + 1) * 512],
                    start=(c == 0), stop=(c == 7)))
                top(lambda e, c=c, tb=tb, half=half: e.matmul(
                    bank(4 + tb)[0:1, :], ONESCOL, half[:, tb * 512:(tb + 1) * 512],
                    start=(c == 0), stop=(c == 7)), after=[stok])
            p.fence()
        for tb in range(4):
            vop(lambda e, tb=tb: e.tensor_copy(out=A2[0:1, tb * 512:(tb + 1) * 512], in_=bank(tb)[0:1, :]))
            vop(lambda e, tb=tb: e.tensor_copy(out=A2[0:1, 2048 + tb * 512: 2048 + (tb + 1) * 512],
                                               in_=bank(4 + tb)[0:1, :]))
        p.fence()
        vop(lambda e: e.tensor_scalar_mul(A2[0:1, 0:2048], A2[0:1, 0:2048], 1.0 / D))
        vop(lambda e: e.tensor_scalar_mul(A2[0:1, 2048:4096], A2[0:1, 2048:4096], 1.0 / D))
        p.fence()
        vop(lambda e: e.tensor_mul(A5[0:1, 0:2048], A2[0:1, 0:2048], A2[0:1, 0:2048]))
        p.fence()
        vop(lambda e: e.tensor_sub(A2[0:1, 2048:4096], A2[0:1, 2048:4096], A5[0:1, 0:2048]))
        p.fence()
        vop(lambda e: e.tensor_scalar_add(A2[0:1, 2048:4096], A2[0:1, 2048:4096], 1e-5))
        p.fence()
        sop(lambda e: e.activation(out=A2[0:1, 2048:4096], in_=A2[0:1, 2048:4096], func=AF.Sqrt))
        p.fence()
        vop(lambda e: e.reciprocal(A2[0:1, 2048:4096], A2[0:1, 2048:4096]))
        p.fence()
        dma("g", d_stat1[0:1, :], A2[0:1, 0:2048])
        dma("g", d_stat1[1:2, :], A2[0:1, 2048:4096])
        p.fence()
        dma("y", A3[:, 0:2048], d_stat1[0:1, :].partition_broadcast(128))
        dma("y", A3[:, 2048:4096], d_stat1[1:2, :].partition_broadcast(128))
        p.fence()
        for c in range(8):
            vop(lambda e, c=c: e.tensor_sub(A0[:, c * 2048:(c + 1) * 2048],
                                            A0[:, c * 2048:(c + 1) * 2048], A3[:, 0:2048]))
        p.fence()
        for c in range(8):
            vop(lambda e, c=c: e.tensor_mul(A0[:, c * 2048:(c + 1) * 2048],
                                            A0[:, c * 2048:(c + 1) * 2048], A3[:, 2048:4096]))
        p.fence()

        cut("LN1")
        dma("y", A5[0:8, 0:1024], gateW)
        p.fence()
        for dc in range(8):
            tr(A5[0:8, dc * 128:(dc + 1) * 128], A5[:, 1024 + dc * 8: 1024 + (dc + 1) * 8])
        p.fence()
        for tb in range(4):
            last = None
            for dc in range(8):
                last = top(lambda e, tb=tb, dc=dc: e.matmul(
                    bank(tb)[0:8, :], A5[:, 1024 + dc * 8: 1024 + (dc + 1) * 8],
                    A0[:, dc * 2048 + tb * 512: dc * 2048 + (tb + 1) * 512],
                    start=(dc == 0), stop=(dc == 7)))
            vop(lambda e, tb=tb: e.tensor_copy(out=A5[0:8, 2048 + tb * 512: 2048 + (tb + 1) * 512],
                                               in_=bank(tb)[0:8, :]), after=[last])
        p.fence()
        dma("g", d_ag_in, A5[0:8, 2048:4096])
        p.fence()
        p.op("g", lambda e: e.collective_compute(
            "AllGather", ALU.bypass, replica_groups=[list(range(8))],
            ins=[d_ag_in[:, :]], outs=[d_ag_out[:, :]]), cc=True)
        p.fence()
        dma("y", A5[0:64, 0:2048], d_ag_out[:, :])
        p.fence()
        for c2 in range(16):
            slot = tr_state["i"] % 2
            tr_state["i"] += 1
            sl = PB[:, 1024 + slot * 512: 1024 + slot * 512 + 128]
            tk = top(lambda e, c2=c2, sl=sl: e.transpose(
                sl[0:128, 0:64], A5[0:64, c2 * 128:(c2 + 1) * 128], IDENT[0:64, 0:64]),
                after=[tr_state["guard"][slot]])
            ev = vop(lambda e, c2=c2, sl=sl: e.tensor_copy(
                out=A4[:, 0:1024].rearrange("p (c1 c2x e) -> p c1 c2x e", c1=8, c2x=16)[:, :, c2, :],
                in_=sl[0:128, 0:64].rearrange("p (c1 e) -> p c1 e", e=8)), after=[tk])
            tr_state["guard"][slot] = ev
        p.fence()
        vop(lambda e: e.tensor_scalar_mul(A4[:, 0:1024], A4[:, 0:1024], -20.0))
        p.fence()
        TL = A4[:, 0:1024]
        EXB = A4[:, 1024:2048]
        MR = A4[:, 2048:2176]
        SR = A4[:, 2176:2304]
        tl3 = TL.rearrange("p (c e) -> p c e", e=8)
        ex3 = EXB.rearrange("p (c e) -> p c e", e=8)
        mr3 = MR.rearrange("p c -> p c ()")
        for it in range(20):
            vop(lambda e: e.reduce_max(mr3, tl3, axis=AX))
            p.fence()
            vop(lambda e: e.tensor_tensor(out=ex3, in0=tl3,
                                          in1=mr3.broadcast_to([128, 128, 8]), op=ALU.subtract))
            p.fence()
            sop(lambda e: e.activation(out=EXB, in_=EXB, func=AF.Exp))
            p.fence()
            vop(lambda e: e.reduce_sum(SR.rearrange("p c -> p c ()"), ex3, axis=AX))
            p.fence()
            sop(lambda e: e.activation(out=SR, in_=SR, func=AF.Ln))
            p.fence()
            vop(lambda e: e.tensor_add(MR, MR, SR))
            p.fence()
            vop(lambda e: e.tensor_tensor(out=tl3, in0=tl3,
                                          in1=mr3.broadcast_to([128, 128, 8]), op=ALU.subtract))
            p.fence()
            vop(lambda e: e.reduce_max(V0[:, 300:308].rearrange("p e -> p e ()"),
                                       TL.rearrange("p (c e) -> p e c", e=8), axis=AX))
            p.fence()
            tr(V0[:, 300:308], V0[0:8, 308:436])
            p.fence()
            vop(lambda e: e.reduce_max(V0[0:8, 436:437], V0[0:8, 308:436], axis=AX))
            p.fence()
            tr(V0[0:8, 436:437], V0[0:1, 437:445])
            p.fence()
            dma("g", d_rep[0:1, 0:8], V0[0:1, 437:445])
            p.fence()
            dma("y", V0[:, 448:456], d_rep[0:1, 0:8].partition_broadcast(128))
            p.fence()
            vop(lambda e: e.tensor_tensor(out=ex3, in0=tl3,
                                          in1=V0[:, 448:456].rearrange("p e -> p () e").broadcast_to([128, 128, 8]),
                                          op=ALU.subtract))
            p.fence()
            sop(lambda e: e.activation(out=EXB, in_=EXB, func=AF.Exp))
            p.fence()
            vop(lambda e: e.reduce_sum(V0[:, 456:464].rearrange("p e -> p e ()"),
                                       EXB.rearrange("p (c e) -> p e c", e=8), axis=AX))
            p.fence()
            top(lambda e: e.matmul(bank(0)[0:8, 0:1], V0[:, 456:464], ONESCOL, start=True, stop=True))
            p.fence()
            vop(lambda e: e.tensor_copy(out=V0[0:8, 464:465], in_=bank(0)[0:8, 0:1]))
            p.fence()
            sop(lambda e: e.activation(out=V0[0:8, 464:465], in_=V0[0:8, 464:465], func=AF.Ln))
            p.fence()
            vop(lambda e: e.tensor_add(V0[0:8, 465:466], V0[0:8, 436:437], V0[0:8, 464:465]))
            p.fence()
            tr(V0[0:8, 465:466], V0[0:1, 466:474])
            p.fence()
            dma("g", d_rep[1:2, 0:8], V0[0:1, 466:474])
            p.fence()
            dma("y", V0[:, 474:482], d_rep[1:2, 0:8].partition_broadcast(128))
            p.fence()
            vop(lambda e: e.tensor_tensor(out=tl3, in0=tl3,
                                          in1=V0[:, 474:482].rearrange("p e -> p () e").broadcast_to([128, 128, 8]),
                                          op=ALU.subtract))
            p.fence()
        PI = A4[:, 1024:2048]
        sop(lambda e: e.activation(out=PI, in_=TL, func=AF.Exp))
        p.fence()
        pi3 = PI.rearrange("p (c e) -> p c e", e=8)
        CAND = A5[:, 0:1024]
        MK1 = A5[:, 1024:2048]
        MK2 = A5[:, 2048:3072]
        TMPB = A5[:, 3072:4096]
        WB = A6[:, 0:1024]
        c3 = lambda buf: buf.rearrange("p (c e) -> p c e", e=8)
        idx3 = IDX8.rearrange("p e -> p () e").broadcast_to([128, 128, 8])
        idxm3 = IDXM8.rearrange("p e -> p () e").broadcast_to([128, 128, 8])
        mrb = mr3.broadcast_to([128, 128, 8])
        vop(lambda e: e.reduce_max(mr3, pi3, axis=AX))
        p.fence()
        vop(lambda e: e.tensor_tensor(out=c3(CAND), in0=pi3, in1=mrb, op=ALU.is_equal))
        p.fence()
        vop(lambda e: e.tensor_tensor(out=c3(CAND), in0=c3(CAND), in1=idxm3, op=ALU.mult))
        p.fence()
        vop(lambda e: e.tensor_scalar_add(CAND, CAND, 8.0))
        p.fence()
        vop(lambda e: e.tensor_reduce(mr3, c3(CAND), axis=AX, op=ALU.min))
        p.fence()
        vop(lambda e: e.tensor_tensor(out=c3(MK1), in0=idx3, in1=mrb, op=ALU.is_equal))
        p.fence()
        vop(lambda e: e.tensor_scalar_mul(TMPB, MK1, 1e30))
        p.fence()
        vop(lambda e: e.tensor_sub(TMPB, PI, TMPB))
        p.fence()
        vop(lambda e: e.reduce_max(mr3, c3(TMPB), axis=AX))
        p.fence()
        vop(lambda e: e.tensor_tensor(out=c3(CAND), in0=c3(TMPB), in1=mrb, op=ALU.is_equal))
        p.fence()
        vop(lambda e: e.tensor_tensor(out=c3(CAND), in0=c3(CAND), in1=idxm3, op=ALU.mult))
        p.fence()
        vop(lambda e: e.tensor_scalar_add(CAND, CAND, 8.0))
        p.fence()
        vop(lambda e: e.tensor_reduce(mr3, c3(CAND), axis=AX, op=ALU.min))
        p.fence()
        vop(lambda e: e.tensor_tensor(out=c3(MK2), in0=idx3, in1=mrb, op=ALU.is_equal))
        p.fence()
        vop(lambda e: e.tensor_tensor(out=c3(WB), in0=c3(MK1),
                                      in1=PI[:, 0::8].rearrange("p c -> p c ()").broadcast_to([128, 128, 8]),
                                      op=ALU.mult))
        p.fence()
        vop(lambda e: e.tensor_tensor(out=c3(TMPB), in0=c3(MK2),
                                      in1=PI[:, 1::8].rearrange("p c -> p c ()").broadcast_to([128, 128, 8]),
                                      op=ALU.mult))
        p.fence()
        vop(lambda e: e.tensor_add(WB, WB, TMPB))
        p.fence()
        vop(lambda e: e.tensor_tensor(
            out=A5[:, 0:1024].rearrange("p (cl e c1) -> p cl e c1", cl=16, e=8),
            in0=WB.rearrange("p (c1 cl e) -> p cl e c1", c1=8, e=8),
            in1=COH.rearrange("p c1 -> p () () c1").broadcast_to([128, 16, 8, 8]),
            op=ALU.mult))
        p.fence()
        vop(lambda e: e.tensor_reduce(
            A5[:, 1024:1152].rearrange("p ce -> p ce ()"),
            A5[:, 0:1024].rearrange("p (ce c1) -> p ce c1", c1=8),
            axis=AX, op=ALU.add))
        p.fence()
        tr(A5[:, 1024:1152], A5[:, 1152:1280])
        p.fence()
        dma("g", d_wtok, A5[:, 1152:1280])
        p.fence()
        if DEBUG:
            dma("g", dbg["dbg_pi"], PI)
            dma("g", dbg["dbg_wtok"], d_wtok)
            p.fence()

        cut("S1")
        for oc in range(16):
            half = A2[:, (oc % 2) * 1024:(oc % 2) * 1024 + 1024]
            lw = ld("y", half, wqkv[:, oc * 128:(oc + 1) * 128], 128)
            ev = []
            for tb in range(4):
                last = None
                for dc in range(8):
                    last = top(lambda e, half=half, tb=tb, dc=dc: e.matmul(
                        bank(tb), half[:, dc * 128:(dc + 1) * 128],
                        A0[:, dc * 2048 + tb * 512: dc * 2048 + (tb + 1) * 512],
                        start=(dc == 0), stop=(dc == 7)), after=[lw])
                ev.append(vop(lambda e, tb=tb: e.tensor_copy(out=A5[:, tb * 512:(tb + 1) * 512],
                                                             in_=bank(tb)), after=[last]))
            dma("g", d_qkT[oc * 128:(oc + 1) * 128, :], A5[:, 0:2048], after=ev)
            p.fence()
        for db in range(2):
            lv = ld("y", A3[:, :], wqkv[:, 2048 + db * 512: 2048 + (db + 1) * 512], 512)
            for tc in range(16):
                last = None
                for dc in range(8):
                    last = top(lambda e, tc=tc, dc=dc: e.matmul(
                        bank(tc % 8), A0[:, dc * 2048 + tc * 128: dc * 2048 + (tc + 1) * 128],
                        A3[:, dc * 512:(dc + 1) * 512],
                        start=(dc == 0), stop=(dc == 7)), after=[lv])
                tk = vop(lambda e, tc=tc: e.tensor_copy(out=A5[:, (tc % 8) * 512:(tc % 8) * 512 + 512],
                                                        in_=bank(tc % 8)), after=[last])
                dma("g", d_vtok[tc * 128:(tc + 1) * 128, db * 512:(db + 1) * 512],
                    A5[:, (tc % 8) * 512:(tc % 8) * 512 + 512], after=[tk])
                p.fence()
        for h in range(16):
            hp, hsel = h // 2, h % 2
            lk = dma("y", A2[0:64, 0:2048], d_qkT[1024 + h * 64: 1024 + (h + 1) * 64, :])
            lq = dma("y", A3[0:64, 0:2048], d_qkT[h * 64:(h + 1) * 64, :])
            lv = ld("y", A2[:, 2048:3072], d_vtok[:, h * 64:(h + 1) * 64], 64)
            p.fence()
            for qt in range(16):
                nk = (qt + 1) * 128
                smm = []
                for kb in range((nk + 511) // 512):
                    n = min(512, nk - kb * 512)
                    smm.append(top(lambda e, qt=qt, kb=kb, n=n: e.matmul(
                        PS[:, kb * 512: kb * 512 + n], A3[0:64, qt * 128:(qt + 1) * 128],
                        A2[0:64, kb * 512: kb * 512 + n], start=True, stop=True), after=[lk, lq]))
                mtok = vop(lambda e, nk=nk: e.tensor_add(PS[:, nk - 128:nk], PS[:, nk - 128:nk], CAUSAL),
                           after=smm)
                mtok = vop(lambda e, nk=nk: e.reduce_max(V0[:, 0:1], PS[:, 0:nk], axis=AX), after=[mtok])
                mtok = vop(lambda e: e.tensor_scalar_mul(V0[:, 1:2], V0[:, 0:1], -0.125), after=[mtok])
                etok = sop(lambda e, nk=nk: e.activation(
                    out=A5[:, 0:nk], in_=PS[:, 0:nk], func=AF.Exp, scale=0.125,
                    bias=V0[:, 1:2], accum_out=V0[:, 2:3]), after=[mtok])
                rtok = vop(lambda e: e.reciprocal(V0[:, 3:4], V0[:, 2:3]), after=[etok])
                ntok = vop(lambda e, nk=nk: e.tensor_scalar(
                    out=A5[:, 0:nk], in0=A5[:, 0:nk], scalar1=V0[:, 3:4], scalar2=None,
                    op0=ALU.mult), after=[rtok])
                pv_last = None
                for kc in range(qt + 1):
                    ttk = tr(A5[:, kc * 128:(kc + 1) * 128], A5[:, 2048 + kc * 128: 2048 + (kc + 1) * 128],
                             after=[ntok])
                    pv_last = top(lambda e, kc=kc, qt=qt, hsel=hsel: e.matmul(
                        PB[0:64, 0:128] if hsel == 0 else PB[64:128, 0:128],
                        A2[:, 2048 + kc * 64: 2048 + (kc + 1) * 64],
                        A5[:, 2048 + kc * 128: 2048 + (kc + 1) * 128],
                        start=(kc == 0), stop=(kc == qt),
                        tile_position=(0, 64 * hsel)), after=[ttk, lv])
                sl = PB[0:64, 0:128] if hsel == 0 else PB[64:128, 0:128]
                dstA6 = A6[0:64, qt * 128:(qt + 1) * 128] if hsel == 0 else A6[64:128, qt * 128:(qt + 1) * 128]
                vop(lambda e, sl=sl, dstA6=dstA6: e.tensor_copy(out=dstA6, in_=sl), after=[pv_last])
                p.fence()
            if hsel == 1:
                dma("g", d_at1T[hp * 128:(hp + 1) * 128, :], A6[:, 0:2048])
                p.fence()
        if DEBUG:
            dma("g", dbg["dbg_at1T"], d_at1T)
        for tt in range(4):
            lat = ld("y", A2[:, :], d_at1T[:, tt * 512:(tt + 1) * 512], 512)
            lx1 = ld("y", A3[:, :], d_x1T[:, tt * 512:(tt + 1) * 512], 512)
            for dc in range(8):
                half = A6[:, (dc % 2) * 1024:(dc % 2) * 1024 + 1024]
                lw = ld("y", half, wout[:, dc * 128:(dc + 1) * 128], 128)
                last = None
                for hc in range(8):
                    last = top(lambda e, half=half, dc=dc, hc=hc: e.matmul(
                        bank(dc), half[:, hc * 128:(hc + 1) * 128],
                        A2[:, hc * 512:(hc + 1) * 512],
                        start=(hc == 0), stop=(hc == 7)), after=[lw, lat])
                vop(lambda e, dc=dc, tt=tt: e.tensor_add(
                    A1[:, dc * 2048 + tt * 512: dc * 2048 + (tt + 1) * 512],
                    bank(dc), A3[:, dc * 512:(dc + 1) * 512]), after=[last, lx1])
                p.fence()
        p.fence()

        cut("A1")
        bgF = [None] * 8
        wrep_last = None
        pcons = [None] * 4
        gscons = [None] * 4
        stg_g = [A2[:, 0:1024], A2[:, 2048:3072]]
        stg_u = [A2[:, 1024:2048], A2[:, 3072:4096]]
        stg_d = [A5[:, 2048:3072], A5[:, 3072:4096]]
        wgt = [A3[:, 0:1024], A3[:, 2048:3072]]
        wut = [A3[:, 1024:2048], A3[:, 3072:4096]]
        wdt = [A4[:, 2048:3072], A4[:, 3072:4096]]
        trg_last = [None, None]
        tru_last = [None, None]
        trd_last = [None, None]
        stg_g_last = [None, None]
        stg_u_last = [None, None]
        stg_d_last = [None, None]
        for ex in range(8):
            wr_toks = []
            for tt in range(4):
                wr_toks.append(dma(
                    "y", A4[:, tt * 512:(tt + 1) * 512].rearrange("p (r q) -> p r q", q=128),
                    d_wtok.rearrange("(cl e) p -> e cl p", e=8)[ex, tt * 4:(tt + 1) * 4, :][None]
                    .partition_broadcast(128), after=[wrep_last]))
            for fs in range(32):
                par = fs % 2
                lwg = dma("y", stg_g[par], Wgate[ex, fs * 128:(fs + 1) * 128, :],
                          after=[stg_g_last[par]])
                lwu = dma("y", stg_u[par], Wup[ex, fs * 128:(fs + 1) * 128, :],
                          after=[stg_u_last[par]])
                lwd = ld("y", stg_d[par], Wdown[ex, :, fs * 128:(fs + 1) * 128], 128,
                         after=[stg_d_last[par]])
                g_evs, u_evs, d_evs = [], [], []
                for dc in range(8):
                    e1 = tr(stg_g[par][:, dc * 128:(dc + 1) * 128],
                            wgt[par][:, dc * 128:(dc + 1) * 128],
                            after=[lwg, trg_last[par]], evac_eng="s")
                    e2 = tr(stg_u[par][:, dc * 128:(dc + 1) * 128],
                            wut[par][:, dc * 128:(dc + 1) * 128],
                            after=[lwu, tru_last[par]], evac_eng="s")
                    e3 = tr(stg_d[par][:, dc * 128:(dc + 1) * 128],
                            wdt[par][:, dc * 128:(dc + 1) * 128],
                            after=[lwd, trd_last[par]], evac_eng="s")
                    g_evs.append(e1); u_evs.append(e2); d_evs.append(e3)
                stg_g_last[par] = g_evs[-1]
                stg_u_last[par] = u_evs[-1]
                stg_d_last[par] = d_evs[-1]
                sg = [None] * 4
                glast = {}
                for dc in range(8):
                    for tb in range(4):
                        glast[tb] = top(lambda e, par=par, tb=tb, dc=dc: e.matmul(
                            bank(tb), wgt[par][:, dc * 128:(dc + 1) * 128],
                            A0[:, dc * 2048 + tb * 512: dc * 2048 + (tb + 1) * 512],
                            start=(dc == 0), stop=(dc == 7)),
                            after=([bgF[0], bgF[1], bgF[2], bgF[3]] + g_evs)
                            if dc == 0 and tb == 0 else [])
                for tb in range(4):
                    sg[tb] = sop(lambda e, tb=tb: e.activation(
                        out=A5[:, tb * 512:(tb + 1) * 512], in_=bank(tb), func=AF.Silu),
                        after=[glast[tb], gscons[tb]])
                    bgF[tb] = sg[tb]
                trg_last[par] = glast[3]
                ulast = {}
                ubmap = [4, 5, 0, 1]
                for dc in range(8):
                    for tb in range(4):
                        ub = ubmap[tb]
                        ulast[tb] = top(lambda e, par=par, tb=tb, dc=dc, ub=ub: e.matmul(
                            bank(ub), wut[par][:, dc * 128:(dc + 1) * 128],
                            A0[:, dc * 2048 + tb * 512: dc * 2048 + (tb + 1) * 512],
                            start=(dc == 0), stop=(dc == 7)),
                            after=([bgF[4], bgF[5], bgF[0], bgF[1]] + u_evs)
                            if dc == 0 and tb == 0 else [])
                for tb in range(4):
                    ub = ubmap[tb]
                    tk = vop(lambda e, tb=tb, ub=ub: e.tensor_mul(
                        A6[:, tb * 512:(tb + 1) * 512], bank(ub),
                        A5[:, tb * 512:(tb + 1) * 512]), after=[ulast[tb], sg[tb], pcons[tb]])
                    gscons[tb] = tk
                    tk2 = vop(lambda e, tb=tb: e.tensor_mul(
                        A6[:, tb * 512:(tb + 1) * 512], A6[:, tb * 512:(tb + 1) * 512],
                        A4[:, tb * 512:(tb + 1) * 512]), after=[tk] + (wr_toks if fs == 0 else []))
                    wrep_last = tk2
                    bgF[ub] = tk2
                tru_last[par] = ulast[3]
                dmm_last = None
                for dc in range(8):
                    for tb in range(4):
                        ub = ubmap[tb]
                        db_ = (dc * 4 + tb) % 6
                        aft = (list(bgF[0:6]) + d_evs) if (dc == 0 and tb == 0) else [bgF[db_]]
                        last = top(lambda e, par=par, tb=tb, dc=dc, db_=db_: e.matmul(
                            bank(db_), wdt[par][:, dc * 128:(dc + 1) * 128],
                            A6[:, tb * 512:(tb + 1) * 512],
                            start=True, stop=True), after=aft)
                        dmm_last = last
                        pcons[tb] = last
                        bgF[db_] = vop(lambda e, dc=dc, tb=tb, db_=db_: e.tensor_add(
                            A1[:, dc * 2048 + tb * 512: dc * 2048 + (tb + 1) * 512],
                            A1[:, dc * 2048 + tb * 512: dc * 2048 + (tb + 1) * 512],
                            bank(db_)), after=[last])
                trd_last[par] = dmm_last
        p.fence()

        for t in range(16):
            toks = [tr(A1[:, c * 2048 + t * 128: c * 2048 + (t + 1) * 128],
                       A2[:, c * 128:(c + 1) * 128]) for c in range(8)]
            dma("g", out_d[t * 128:(t + 1) * 128, :], A2[:, 0:1024], after=toks)
            p.fence()

    except _DoneEmit:
        pass

    with ExitStack() as es2:
        for ev in Prog.EVENTS:
            p.sems[ev] = es2.enter_context(nc.semaphore("sem_" + ev))
        with nc.Block() as block:
            @block.tensor
            def _(engine):
                p.replay("t", engine)

            @block.vector
            def _(engine):
                p.replay("v", engine)

            @block.scalar
            def _(engine):
                p.replay("s", engine)

            @block.sync
            def _(engine):
                p.replay("y", engine)

            @block.gpsimd
            def _(engine):
                p.replay("g", engine)
                p.final_waits("g", engine)
        es.close()
    return nc


def make_in_maps(inputs):
    cst = host_constants()
    base = {
        "Wq": inputs["L0_attn_Wq"], "Wk": inputs["L0_attn_Wk"], "Wv": inputs["L0_attn_Wv"],
        "bq": inputs["L0_attn_bq"].reshape(1, D), "bk": inputs["L0_attn_bk"].reshape(1, D),
        "bv": inputs["L0_attn_bv"].reshape(1, D),
        "gamma": inputs["L0_ffn_ln_gamma"].reshape(1, D), "beta": inputs["L0_ffn_ln_beta"].reshape(1, D),
        "mu": np.asarray(inputs["L0_ffn_mu"]).reshape(D, 256),
        "f0scale": inputs["L0_ffn_scale"].reshape(1, 1),
        "w1": inputs["L0_ffn_w1"], "b1": inputs["L0_ffn_b1"],
        "w2": inputs["L0_ffn_w2"], "b2": inputs["L0_ffn_b2"],
        "wqkv": inputs["L1_attn_w_qkv"], "wout": inputs["L1_attn_w_out"],
        "gateW": inputs["L1_ffn_gate_W"],
        "Wgate": inputs["L1_ffn_W_gate"], "Wup": inputs["L1_ffn_W_up"],
        "Wdown": inputs["L1_ffn_W_down"],
    }
    base = {k: np.ascontiguousarray(np.asarray(v, dtype=np.float32)) for k, v in base.items()}
    base.update(cst)
    maps = []
    x = np.asarray(inputs["L0_attn_x"], dtype=np.float32)
    for c in range(8):
        m = dict(base)
        m["x"] = np.ascontiguousarray(x[c])
        oh = np.zeros((1, 8), np.float32)
        oh[0, c] = 1.0
        m["coreoh"] = oh
        maps.append(m)
    return maps


def kernel(**inputs):
    nc = build_program()
    maps = make_in_maps(inputs)
    res = run_bass_kernel_spmd(nc, maps, core_ids=list(range(8)))
    return np.stack([np.asarray(res.results[c]["out"]) for c in range(8)], axis=0)


# version 04
# speedup vs baseline: 1.0051x; 1.0051x over previous
import numpy as np
import concourse.bass as bass
import concourse.mybir as mybir
from concourse.bass_utils import run_bass_kernel_spmd
from contextlib import ExitStack

F32 = mybir.dt.float32
AF = mybir.ActivationFunctionType
ALU = mybir.AluOpType
AX = mybir.AxisListType.X

T = 2048
D = 1024
NF = 512
DEBUG = False
STOP_AT = None


class _DoneEmit(Exception):
    pass


class Prog:
    ENGS = ("t", "v", "s", "y", "g")
    EVENTS = ("t", "v", "s", "y", "g", "dy", "dg", "ds", "cc")

    def __init__(self):
        self.ops = {e: [] for e in self.ENGS}
        self.count = {e: 0 for e in self.EVENTS}
        self.hwm = {e: {} for e in self.ENGS}
        self.fence_snap = []
        self.sems = {}

    def op(self, eng, fn, after=(), dma=False, cc=False):
        waits = list(self.fence_snap)
        for a in after:
            if a is not None:
                waits.append(a)
        if cc:
            ev, amt = "cc", 1
        elif dma:
            ev, amt = "d" + eng, 16
        else:
            ev, amt = eng, 1
        self.count[ev] += amt
        tok = (ev, self.count[ev])
        self.ops[eng].append((fn, waits, (ev, amt)))
        return tok

    def fence(self):
        self.fence_snap = [(ev, c) for ev, c in self.count.items() if c > 0]

    def replay(self, eng, engine):
        h = self.hwm[eng]
        n_own = 0
        for fn, waits, (ev, amt) in self.ops[eng]:
            if eng in ("v", "s") and n_own > 0:
                if h.get(eng, 0) < n_own:
                    engine.wait_ge(self.sems[eng], n_own)
                    h[eng] = n_own
            for wev, val in waits:
                if h.get(wev, 0) < val:
                    engine.wait_ge(self.sems[wev], val)
                    h[wev] = val
            fn(engine).then_inc(self.sems[ev], amt)
            if ev == eng:
                n_own = max(n_own, self.count_at(eng, fn) if False else 0)
            n_own += amt if ev == eng else 0

    def final_waits(self, eng, engine):
        h = self.hwm[eng]
        for ev, c in self.count.items():
            if c > 0 and h.get(ev, 0) < c:
                engine.wait_ge(self.sems[ev], c)
                h[ev] = c


def host_constants():
    d = np.arange(D, dtype=np.float64)
    k = np.arange(NF, dtype=np.float64)
    ang = 2.0 * np.pi * np.outer(d, k) / D
    C = np.cos(ang)
    S = -np.sin(ang)
    S[:, 0] = (-1.0) ** d
    kd = 2.0 * np.pi * np.outer(k, d) / D
    C2 = 2.0 * np.cos(kd) / D
    C2[0, :] = 1.0 / D
    S2 = -2.0 * np.sin(kd) / D
    S2[0, :] = ((-1.0) ** d) / D
    causal = np.where(np.arange(128)[:, None] >= np.arange(128)[None, :], 0.0, -1e10)
    return {
        "cst_C": np.ascontiguousarray(C.astype(np.float32)),
        "cst_S": np.ascontiguousarray(S.astype(np.float32)),
        "cst_C2": np.ascontiguousarray(C2.astype(np.float32)),
        "cst_S2": np.ascontiguousarray(S2.astype(np.float32)),
        "cst_ident": np.eye(128, dtype=np.float32),
        "cst_causal": np.ascontiguousarray(causal.astype(np.float32)),
        "cst_idxm8": (np.arange(8, dtype=np.float32) - 8.0).reshape(1, 8),
        "cst_idx8": np.arange(8, dtype=np.float32).reshape(1, 8),
    }


def build_program(stop_at=None):
    nc = bass.Bass("TRN2", target_bir_lowering=False)

    def din(name, shape):
        return nc.dram_tensor(name, shape, F32, kind="ExternalInput").ap()

    x_in = din("x", [T, D])
    Wq, Wk, Wv = din("Wq", [D, D]), din("Wk", [D, D]), din("Wv", [D, D])
    bq, bk, bv = din("bq", [1, D]), din("bk", [1, D]), din("bv", [1, D])
    gamma, beta = din("gamma", [1, D]), din("beta", [1, D])
    mu_in = din("mu", [D, 256]); f0scale = din("f0scale", [1, 1])
    w1_in, b1_in = din("w1", [16, D, 256]), din("b1", [16, 256])
    w2_in, b2_in = din("w2", [16, 256, D]), din("b2", [16, D])
    wqkv, wout = din("wqkv", [D, 3 * D]), din("wout", [D, D])
    gateW = din("gateW", [8, D])
    Wgate, Wup = din("Wgate", [8, 4096, D]), din("Wup", [8, 4096, D])
    Wdown = din("Wdown", [8, D, 4096])
    coreoh = din("coreoh", [1, 8])
    Cm, Sm = din("cst_C", [D, NF]), din("cst_S", [D, NF])
    C2m, S2m = din("cst_C2", [NF, D]), din("cst_S2", [NF, D])
    identD, causalD = din("cst_ident", [128, 128]), din("cst_causal", [128, 128])
    idxm8D, idx8D = din("cst_idxm8", [1, 8]), din("cst_idx8", [1, 8])

    out_d = nc.dram_tensor("out", [T, D], F32, kind="ExternalOutput").ap()

    scr = lambda name, shape, **kw: nc.dram_tensor(name, shape, F32, **kw).ap()
    d_xT = scr("d_xT", [D, T])
    d_xnT = scr("d_xnT", [D, T])
    d_xntok = scr("d_xntok", [T, D])
    d_G = scr("d_G", [6, D, NF])
    d_a0T = scr("d_a0T", [D, T])
    d_xn2T = scr("d_xn2T", [D, T])
    d_xn2tok = scr("d_xn2tok", [T, D])
    d_rl = scr("d_rl", [1, T])
    d_x1T = scr("d_x1T", [D, T])
    d_stat1 = scr("d_stat1", [2, T])
    d_qkT = scr("d_qkT", [2 * D, T])
    d_vtok = scr("d_vtok", [T, D])
    d_at1T = scr("d_at1T", [D, T])
    d_wtok = scr("d_wtok", [128, 128])
    d_rep = scr("d_rep", [4, T])
    d_ag_in = scr("d_ag_in", [8, T])
    d_ag_out = scr("d_ag_out", [64, T], addr_space="Shared")

    dbg = {}
    if DEBUG:
        for nm, shp in [("dbg_V0", [128, 64]), ("dbg_xntok", [T, D]), ("dbg_xnT", [D, T]), ("dbg_a0T", [D, T]), ("dbg_x1T", [D, T]),
                        ("dbg_at1T", [D, T]), ("dbg_pi", [128, 1024]), ("dbg_wtok", [128, 128]),
                        ("dbg_xn2T", [D, T])]:
            dbg[nm] = nc.dram_tensor(nm, shp, F32, kind="ExternalOutput").ap()

    es = ExitStack()
    sb = lambda nm, cols: es.enter_context(nc.sbuf_tensor(nm, [128, cols], F32))
    A0 = sb("A0", 16384)
    A1 = sb("A1", 16384)
    A2 = sb("A2", 4096)
    A3 = sb("A3", 4096)
    A4 = sb("A4", 4096)
    A5 = sb("A5", 4096)
    A6 = sb("A6", 2624)
    V0 = sb("V0", 512)
    CS_ = sb("CST", 896)
    PS = es.enter_context(nc.psum_tensor("PS", [128, 2048], F32))
    PB = es.enter_context(nc.psum_tensor("PB", [128, 2048], F32))

    IDENT = CS_[:, 0:128]
    CAUSAL = CS_[:, 128:256]
    ONES512 = CS_[:, 256:768]
    BT = CS_[:, 768:792]
    IDXM8 = CS_[:, 800:808]
    IDX8 = CS_[:, 808:816]
    CARRY = CS_[:, 816:824]
    ONESCOL = CS_[:, 824:825]
    COH = CS_[:, 832:840]

    def bank(i):
        t_ = PS if i < 4 else PB
        return t_[:, (i % 4) * 512:(i % 4 + 1) * 512]

    p = Prog()

    def dma(eng, out, in_, after=()):
        return p.op(eng, lambda e, o=out, i=in_: e.dma_start(out=o, in_=i), after=after, dma=True)

    def dma_slow(eng, out, in_, after=()):
        return p.op(eng, lambda e, o=out, i=in_: e.dma_start(
            out=o, in_=i, allow_slow_non_contiguous=True), after=after, dma=True)


    def cut(name):
        if stop_at == name:
            if DEBUG:
                p.fence()
                dma("g", dbg["dbg_V0"], V0[:, 0:64])
                p.fence()
            raise _DoneEmit()

    def ld(eng, sbuf_slice, dram_ap, inner, after=()):
        src3 = dram_ap.rearrange("(n p) d -> p n d", p=128)
        dst3 = sbuf_slice.rearrange("p (n d) -> p n d", d=inner)
        return dma(eng, dst3, src3, after=after)

    def st(eng, dram_ap, sbuf_slice, inner, after=()):
        dst3 = dram_ap.rearrange("(n p) d -> p n d", p=128)
        src3 = sbuf_slice.rearrange("p (n d) -> p n d", d=inner)
        return dma(eng, dst3, src3, after=after)

    vop = lambda fn, after=(): p.op("v", fn, after=after)
    sop = lambda fn, after=(): p.op("s", fn, after=after)
    top = lambda fn, after=(): p.op("t", fn, after=after)

    tr_state = {"i": 0, "guard": [None, None]}

    def tr(src, dst, after=(), evac_eng="v"):
        slot = tr_state["i"] % 2
        tr_state["i"] += 1
        p0, f0 = src.shape[0], src.shape[1]
        sl = PB[:, 1024 + slot * 512: 1024 + slot * 512 + 128]
        deps = list(after) + [tr_state["guard"][slot]]
        tk = top(lambda e, s=src, sl=sl, p0=p0, f0=f0: e.transpose(
            sl[0:f0, 0:p0], s, IDENT[0:p0, 0:p0]), after=deps)
        if evac_eng == "v":
            ev = vop(lambda e, d=dst, sl=sl, p0=p0, f0=f0: e.tensor_copy(
                out=d, in_=sl[0:f0, 0:p0]), after=[tk])
        else:
            ev = sop(lambda e, d=dst, sl=sl, p0=p0, f0=f0: e.activation(
                out=d, in_=sl[0:f0, 0:p0], func=AF.Copy), after=[tk])
        tr_state["guard"][slot] = ev
        return ev

    try:
        dma("y", IDENT, identD)
        dma("y", CAUSAL, causalD)
        dma("y", IDXM8, idxm8D.partition_broadcast(128))
        dma("y", IDX8, idx8D.partition_broadcast(128))
        dma("y", COH, coreoh.partition_broadcast(128))
        vop(lambda e: e.memset(ONES512, 1.0))
        vop(lambda e: e.memset(ONESCOL, 1.0))
        vop(lambda e: e.memset(CARRY, 0.0))

        cut("C0")
        ld("y", A0[:, :], x_in, D)
        p.fence()
        for t in range(16):
            xt = A0[:, t * D:(t + 1) * D]
            vop(lambda e, xt=xt, t=t: e.reduce_sum(V0[:, t:t + 1], xt, axis=AX))
            vop(lambda e, xt=xt, t=t: e.tensor_mul(A1[:, t * D:(t + 1) * D], xt, xt))
        p.fence()
        for t in range(16):
            vop(lambda e, t=t: e.reduce_sum(V0[:, 16 + t:17 + t], A1[:, t * D:(t + 1) * D], axis=AX))
        p.fence()
        cut("INsums")
        vop(lambda e: e.tensor_scalar_mul(V0[:, 0:16], V0[:, 0:16], 1.0 / D))
        vop(lambda e: e.tensor_scalar_mul(V0[:, 16:32], V0[:, 16:32], 1.0 / D))
        vop(lambda e: e.tensor_mul(V0[:, 32:48], V0[:, 0:16], V0[:, 0:16]))
        vop(lambda e: e.tensor_sub(V0[:, 16:32], V0[:, 16:32], V0[:, 32:48]))
        p.fence()
        vop(lambda e: e.tensor_scalar_add(V0[:, 16:32], V0[:, 16:32], 1e-5))
        p.fence()
        sop(lambda e: e.activation(out=V0[:, 32:48], in_=V0[:, 16:32], func=AF.Sqrt))
        p.fence()
        vop(lambda e: e.reciprocal(V0[:, 48:64], V0[:, 32:48]))
        p.fence()
        for t in range(16):
            vop(lambda e, t=t: e.tensor_scalar(
                out=A1[:, t * D:(t + 1) * D], in0=A0[:, t * D:(t + 1) * D],
                scalar1=V0[:, t:t + 1], scalar2=V0[:, 48 + t:49 + t],
                op0=ALU.subtract, op1=ALU.mult))
        p.fence()
        st("g", d_xntok, A1[:, :], D)
        if DEBUG:
            p.fence()
            dma("g", dbg["dbg_xntok"], d_xntok)
            dma("g", dbg["dbg_V0"], V0[:, 0:64])
            p.fence()
        cut("IN1")
        for (src, dst) in ((A1, d_xnT), (A0, d_xT)):
            for c in range(8):
                toks = [tr(src[:, t * D + c * 128: t * D + (c + 1) * 128],
                           A2[:, t * 128:(t + 1) * 128]) for t in range(16)]
                dma("g", dst[c * 128:(c + 1) * 128, :], A2[:, 0:2048], after=toks)
                p.fence()
        if DEBUG:
            dma("g", dbg["dbg_xnT"], d_xnT)
            dma("g", dbg["dbg_xntok"], d_xntok)

        cut("IN")
        ld("y", A2[:, :], Cm, 512)
        ld("y", A3[:, :], Sm, 512)
        p.fence()
        for sig, (Wm, bm) in enumerate(((Wq, bq), (Wk, bk), (Wv, bv))):
            dma_slow("y", V0[:, 0:8], bm.rearrange("o (c p) -> p (o c)", p=128))
            p.fence()
            for comp, CSbuf in enumerate((A2, A3)):
                sc = sig * 2 + comp
                last = {}
                for ec in range(8):
                    half = A5[:, (ec % 2) * D:(ec % 2) * D + D]
                    lt = dma("y", half, Wm[ec * 128:(ec + 1) * 128, :])
                    for dc in range(8):
                        last[dc] = top(lambda e, half=half, dc=dc, ec=ec, CSbuf=CSbuf: e.matmul(
                            bank(dc), half[:, dc * 128:(dc + 1) * 128],
                            CSbuf[:, ec * 512:(ec + 1) * 512],
                            start=(ec == 0), stop=(ec == 7)), after=[lt])
                    p.fence()
                ev = [vop(lambda e, dc=dc: e.tensor_copy(out=A4[:, dc * 512:(dc + 1) * 512],
                                                         in_=bank(dc)), after=[last[dc]])
                      for dc in range(8)]
                st("g", d_G[sc], A4[:, 0:4096], 512, after=ev)
                p.fence()
                blast = None
                for ec in range(8):
                    blast = top(lambda e, ec=ec, CSbuf=CSbuf: e.matmul(
                        bank(0)[0:1, :], V0[:, ec:ec + 1], CSbuf[:, ec * 512:(ec + 1) * 512],
                        start=(ec == 0), stop=(ec == 7)))
                bev = vop(lambda e: e.tensor_copy(out=A6[0:1, 0:512], in_=bank(0)[0:1, :]), after=[blast])
                dmab = dma("g", d_rep[2:3, 0:512], A6[0:1, 0:512], after=[bev])
                dma_slow("y", BT[:, sc * 4:(sc + 1) * 4],
                    d_rep[2:3, 0:512].rearrange("o (c p) -> p (o c)", p=128), after=[dmab])
                p.fence()

        cut("G")
        def spv(sc, fc):
            return A0[:, sc * 2048 + fc * 512: sc * 2048 + (fc + 1) * 512]
        TMP1 = A0[:, 12288:12800]; TMP2 = A0[:, 12800:13312]
        RK = A0[:, 13312:13824]; RV = A0[:, 13824:14336]; RQ = A0[:, 14336:14848]
        BCB = A0[:, 14848:15360]; BSB = A0[:, 15360:15872]
        R0 = A1[0:1, :]

        for tt in range(4):
            ltx = ld("y", A3[:, :], d_xnT[:, tt * 512:(tt + 1) * 512], 512)
            for sc in range(6):
                lg = ld("y", A2[:, :], d_G[sc], 512)
                mm_last = {}
                for fc in range(4):
                    for dc in range(8):
                        mm_last[fc] = top(lambda e, fc=fc, dc=dc: e.matmul(
                            bank(fc), A2[:, dc * 512 + fc * 128: dc * 512 + (fc + 1) * 128],
                            A3[:, dc * 512:(dc + 1) * 512],
                            start=(dc == 0), stop=(dc == 7)), after=[lg, ltx])
                for fc in range(4):
                    vop(lambda e, sc=sc, fc=fc: e.tensor_scalar(
                        out=spv(sc, fc), in0=bank(fc), scalar1=BT[:, sc * 4 + fc: sc * 4 + fc + 1],
                        scalar2=None, op0=ALU.add), after=[mm_last[fc]])
                p.fence()
            for fc in range(4):
                QC, QS = spv(0, fc), spv(1, fc)
                KC, KS = spv(2, fc), spv(3, fc)
                VC, VS = spv(4, fc), spv(5, fc)
                if fc == 0:
                    r0q_c, r0q_s = R0[:, 0:512], R0[:, 512:1024]
                    r0b_c, r0b_s = R0[:, 1024:1536], R0[:, 1536:2048]
                    t1, t2 = R0[:, 2048:2560], R0[:, 2560:3072]
                    for (Xr, dstn) in ((QC, r0q_c), (QS, r0q_s)):
                        atk = sop(lambda e, Xr=Xr, t1=t1: e.activation(out=t1, in_=Xr[0:1, :], func=AF.Abs))
                        atk = vop(lambda e, t1=t1: e.tensor_scalar_add(t1, t1, 1e-8), after=[atk])
                        atk = vop(lambda e, t2=t2, t1=t1: e.reciprocal(t2, t1), after=[atk])
                        vop(lambda e, Xr=Xr, dstn=dstn, t2=t2: e.tensor_mul(dstn, Xr[0:1, :], t2), after=[atk])
                        p.fence()
                    for (Kr, Vr, dstn) in ((KC, VC, r0b_c), (KS, VS, r0b_s)):
                        a1t = sop(lambda e, Kr=Kr, t1=t1: e.activation(out=t1, in_=Kr[0:1, :], func=AF.Abs))
                        a1t = vop(lambda e, t1=t1: e.tensor_scalar_add(t1, t1, 1e-8), after=[a1t])
                        a2t = sop(lambda e, Vr=Vr, t2=t2: e.activation(out=t2, in_=Vr[0:1, :], func=AF.Abs))
                        a2t = vop(lambda e, t2=t2: e.tensor_scalar_add(t2, t2, 1e-8), after=[a2t])
                        p.fence()
                        vop(lambda e, t1=t1, t2=t2: e.tensor_mul(t1, t1, t2))
                        p.fence()
                        vop(lambda e, t2=t2, t1=t1: e.reciprocal(t2, t1))
                        vop(lambda e, Kr=Kr, Vr=Vr, t1=t1: e.tensor_mul(t1, Kr[0:1, :], Vr[0:1, :]))
                        p.fence()
                        vop(lambda e, dstn=dstn, t1=t1, t2=t2: e.tensor_mul(dstn, t1, t2))
                        p.fence()
                for (Xc, Xs, R) in ((QC, QS, RQ), (KC, KS, RK), (VC, VS, RV)):
                    vop(lambda e, Xc=Xc: e.tensor_mul(TMP1, Xc, Xc))
                    vop(lambda e, Xs=Xs: e.tensor_mul(TMP2, Xs, Xs))
                    p.fence()
                    vop(lambda e: e.tensor_add(TMP1, TMP1, TMP2))
                    p.fence()
                    sop(lambda e: e.activation(out=TMP2, in_=TMP1, func=AF.Sqrt))
                    p.fence()
                    vop(lambda e: e.tensor_scalar_add(TMP2, TMP2, 1e-8))
                    p.fence()
                    vop(lambda e, R=R: e.reciprocal(R, TMP2))
                    p.fence()
                vop(lambda e, QC=QC: e.tensor_mul(QC, QC, RQ))
                vop(lambda e, QS=QS: e.tensor_mul(QS, QS, RQ))
                vop(lambda e: e.tensor_mul(RK, RK, RV))
                p.fence()
                vop(lambda e, KC=KC, VC=VC: e.tensor_mul(TMP1, KC, VC))
                vop(lambda e, KS=KS, VS=VS: e.tensor_mul(TMP2, KS, VS))
                p.fence()
                vop(lambda e: e.tensor_sub(BCB, TMP1, TMP2))
                vop(lambda e, KC=KC, VS=VS: e.tensor_mul(TMP1, KC, VS))
                vop(lambda e, KS=KS, VC=VC: e.tensor_mul(TMP2, KS, VC))
                p.fence()
                vop(lambda e: e.tensor_add(BSB, TMP1, TMP2))
                p.fence()
                vop(lambda e: e.tensor_mul(BCB, BCB, RK))
                vop(lambda e: e.tensor_mul(BSB, BSB, RK))
                p.fence()
                if fc == 0:
                    vop(lambda e, QC=QC: e.tensor_copy(out=QC[0:1, :], in_=R0[:, 0:512]))
                    vop(lambda e, QS=QS: e.tensor_copy(out=QS[0:1, :], in_=R0[:, 512:1024]))
                    vop(lambda e: e.tensor_copy(out=BCB[0:1, :], in_=R0[:, 1024:1536]))
                    vop(lambda e: e.tensor_copy(out=BSB[0:1, :], in_=R0[:, 1536:2048]))
                    p.fence()
                vop(lambda e, KC=KC, fc=fc: e.tensor_tensor_scan(
                    out=KC, data0=ONES512, data1=BCB, initial=CARRY[:, fc * 2:fc * 2 + 1],
                    op0=ALU.mult, op1=ALU.add))
                vop(lambda e, KS=KS, fc=fc: e.tensor_tensor_scan(
                    out=KS, data0=ONES512, data1=BSB, initial=CARRY[:, fc * 2 + 1:fc * 2 + 2],
                    op0=ALU.mult, op1=ALU.add))
                p.fence()
                vop(lambda e, KC=KC, fc=fc: e.tensor_copy(out=CARRY[:, fc * 2:fc * 2 + 1], in_=KC[:, 511:512]))
                vop(lambda e, KS=KS, fc=fc: e.tensor_copy(out=CARRY[:, fc * 2 + 1:fc * 2 + 2], in_=KS[:, 511:512]))
                vop(lambda e, KC=KC, QC=QC: e.tensor_mul(TMP1, KC, QC))
                vop(lambda e, KS=KS, QS=QS: e.tensor_mul(TMP2, KS, QS))
                p.fence()
                vop(lambda e, VC=VC: e.tensor_add(VC, TMP1, TMP2))
                vop(lambda e, KS=KS, QC=QC: e.tensor_mul(TMP1, KS, QC))
                vop(lambda e, KC=KC, QS=QS: e.tensor_mul(TMP2, KC, QS))
                p.fence()
                vop(lambda e, VS=VS: e.tensor_sub(VS, TMP1, TMP2))
                p.fence()
                if fc == 0:
                    vop(lambda e, VC=VC, KC=KC, QC=QC: e.tensor_mul(VC[0:1, :], KC[0:1, :], QC[0:1, :]))
                    vop(lambda e, VS=VS, KS=KS, QS=QS: e.tensor_mul(VS[0:1, :], KS[0:1, :], QS[0:1, :]))
                    p.fence()
            lc2 = ld("y", A2[:, :], C2m, D)
            ls2 = ld("y", A3[:, :], S2m, D)
            ev_toks = []
            for dc in range(8):
                last = None
                for fc in range(4):
                    last = top(lambda e, dc=dc, fc=fc: e.matmul(
                        bank(dc), A2[:, fc * D + dc * 128: fc * D + (dc + 1) * 128], spv(4, fc),
                        start=(fc == 0), stop=False), after=[lc2, ls2])
                for fc in range(4):
                    last = top(lambda e, dc=dc, fc=fc: e.matmul(
                        bank(dc), A3[:, fc * D + dc * 128: fc * D + (dc + 1) * 128], spv(5, fc),
                        start=False, stop=(fc == 3)))
                ev_toks.append(vop(lambda e, dc=dc: e.tensor_copy(
                    out=A4[:, dc * 512:(dc + 1) * 512], in_=bank(dc)), after=[last]))
            st("g", d_a0T[:, tt * 512:(tt + 1) * 512], A4[:, 0:4096], 512, after=ev_toks)
            p.fence()
        if DEBUG:
            dma("g", dbg["dbg_a0T"], d_a0T)

        cut("A0")
        ld("y", A0[:, :], d_xntok, D)
        dma("y", A2[:, 0:D], gamma.partition_broadcast(128))
        dma("y", A2[:, D:2 * D], beta.partition_broadcast(128))
        p.fence()
        for t in range(16):
            xt = A0[:, t * D:(t + 1) * D]
            half = A5[:, 0:D] if t % 2 == 0 else A5[:, D:2 * D]
            stok = sop(lambda e, xt=xt, half=half: e.activation(out=half, in_=xt, func=AF.Square))
            vop(lambda e, t=t, half=half: e.reduce_sum(V0[:, t:t + 1], half, axis=AX), after=[stok])
            p.fence()
        vop(lambda e: e.tensor_scalar_mul(V0[:, 0:16], V0[:, 0:16], 1.0 / D))
        p.fence()
        vop(lambda e: e.tensor_scalar_add(V0[:, 0:16], V0[:, 0:16], 1e-5))
        p.fence()
        sop(lambda e: e.activation(out=V0[:, 16:32], in_=V0[:, 0:16], func=AF.Sqrt))
        p.fence()
        vop(lambda e: e.reciprocal(V0[:, 32:48], V0[:, 16:32]))
        p.fence()
        for t in range(16):
            xt = A0[:, t * D:(t + 1) * D]
            vop(lambda e, xt=xt, t=t: e.tensor_scalar_mul(xt, xt, V0[:, 32 + t:33 + t]))
        p.fence()
        for t in range(16):
            xt = A0[:, t * D:(t + 1) * D]
            vop(lambda e, xt=xt: e.tensor_mul(xt, xt, A2[:, 0:D]))
        p.fence()
        for t in range(16):
            xt = A0[:, t * D:(t + 1) * D]
            vop(lambda e, xt=xt: e.tensor_add(xt, xt, A2[:, D:2 * D]))
        p.fence()
        st("g", d_xn2tok, A0[:, :], D)
        for t in range(16):
            xt = A0[:, t * D:(t + 1) * D]
            half = A5[:, 0:D] if t % 2 == 0 else A5[:, D:2 * D]
            stok = sop(lambda e, xt=xt, half=half: e.activation(out=half, in_=xt, func=AF.Square))
            vop(lambda e, t=t, half=half: e.reduce_sum(V0[:, 64 + t:65 + t], half, axis=AX), after=[stok])
            p.fence()
        sop(lambda e: e.activation(out=V0[:, 80:96], in_=V0[:, 64:80], func=AF.Sqrt))
        p.fence()
        vop(lambda e: e.tensor_scalar_max(V0[:, 80:96], V0[:, 80:96], 1e-12))
        p.fence()
        vop(lambda e: e.reciprocal(V0[:, 96:112], V0[:, 80:96]))
        p.fence()
        dma("y", V0[:, 129:130], f0scale.partition_broadcast(128))
        p.fence()
        vop(lambda e: e.tensor_scalar(out=V0[:, 96:112], in0=V0[:, 96:112],
                                      scalar1=V0[:, 129:130], scalar2=None, op0=ALU.mult))
        p.fence()
        dma_slow("g", d_rl.rearrange("o (n p) -> p (o n)", p=128), V0[:, 96:112])
        p.fence()
        dma("y", A6[:, 0:2048], d_rl.partition_broadcast(128))
        for c in range(8):
            toks = [tr(A0[:, t * D + c * 128: t * D + (c + 1) * 128],
                       A5[:, t * 128:(t + 1) * 128]) for t in range(16)]
            dma("g", d_xn2T[c * 128:(c + 1) * 128, :], A5[:, 0:2048], after=toks)
            p.fence()
        if DEBUG:
            dma("g", dbg["dbg_xn2T"], d_xn2T)
        cut("F0a")
        ld("y", A3[:, 0:2048], mu_in, 256)
        p.fence()
        for c in range(8):
            sop(lambda e, c=c: e.activation(out=A5[:, c * 256:(c + 1) * 256],
                                            in_=A3[:, c * 256:(c + 1) * 256], func=AF.Square))
        p.fence()
        last = None
        for c in range(8):
            last = top(lambda e, c=c: e.matmul(bank(0)[0:1, 0:256], ONESCOL,
                                               A5[:, c * 256:(c + 1) * 256],
                                               start=(c == 0), stop=(c == 7)))
        p.fence()
        sop(lambda e: e.activation(out=A5[0:1, 0:256], in_=bank(0)[0:1, 0:256], func=AF.Sqrt))
        p.fence()
        vop(lambda e: e.tensor_scalar_max(A5[0:1, 0:256], A5[0:1, 0:256], 1e-12))
        p.fence()
        vop(lambda e: e.reciprocal(A5[0:1, 0:256], A5[0:1, 0:256]))
        p.fence()
        dma("g", d_rep[3:4, 0:256], A5[0:1, 0:256])
        p.fence()
        dma("y", A3[:, 2048:2304], d_rep[3:4, 0:256].partition_broadcast(128))
        p.fence()
        for c in range(8):
            vop(lambda e, c=c: e.tensor_mul(A3[:, c * 256:(c + 1) * 256],
                                            A3[:, c * 256:(c + 1) * 256], A3[:, 2048:2304]))
        p.fence()
        cut("F0b")
        for tt in range(4):
            ltk = ld("y", A2[:, :], d_xn2T[:, tt * 512:(tt + 1) * 512], 512)
            for ec in range(2):
                last = None
                for dc in range(8):
                    last = top(lambda e, ec=ec, dc=dc: e.matmul(
                        bank(ec), A3[:, dc * 256 + ec * 128: dc * 256 + (ec + 1) * 128],
                        A2[:, dc * 512:(dc + 1) * 512],
                        start=(dc == 0), stop=(dc == 7)), after=[ltk])
                vop(lambda e, ec=ec, tt=tt: e.tensor_mul(
                    A1[:, ec * 2048 + tt * 512: ec * 2048 + (tt + 1) * 512],
                    bank(ec), A6[:, tt * 512:(tt + 1) * 512]), after=[last])
            p.fence()
        for ec in range(2):
            lt = A1[:, ec * 2048:(ec + 1) * 2048]
            dt_ = A1[:, 4096 + ec * 2048: 4096 + (ec + 1) * 2048]
            vop(lambda e, lt=lt: e.reduce_max(V0[:, 200:201], lt, axis=AX))
            p.fence()
            vop(lambda e: e.tensor_scalar_mul(V0[:, 201:202], V0[:, 200:201], -1.0))
            p.fence()
            sop(lambda e, lt=lt, dt_=dt_: e.activation(out=dt_, in_=lt, func=AF.Exp,
                                                       bias=V0[:, 201:202], accum_out=V0[:, 202:203]))
            p.fence()
            vop(lambda e: e.reciprocal(V0[:, 203:204], V0[:, 202:203]))
            p.fence()
            vop(lambda e, dt_=dt_: e.tensor_scalar(out=dt_, in0=dt_, scalar1=V0[:, 203:204],
                                                   scalar2=None, op0=ALU.mult))
            p.fence()
        for t in range(16):
            for c in range(2):
                tr(A1[:, 4096 + c * 2048 + t * 128: 4096 + c * 2048 + (t + 1) * 128],
                   A2[:, t * 256 + c * 128: t * 256 + (c + 1) * 128])
        p.fence()
        for t in range(16):
            for c in range(2):
                tr(A1[:, c * 2048 + t * 128: c * 2048 + (t + 1) * 128],
                   A1[:, 8192 + t * 256 + c * 128: 8192 + t * 256 + (c + 1) * 128])
        p.fence()
        for t in range(16):
            lt = A1[:, 8192 + t * 256: 8192 + (t + 1) * 256]
            vop(lambda e, lt=lt: e.reduce_max(V0[:, 210:211], lt, axis=AX))
            p.fence()
            vop(lambda e: e.tensor_scalar_mul(V0[:, 211:212], V0[:, 210:211], -1.0))
            p.fence()
            sop(lambda e, lt=lt: e.activation(out=lt, in_=lt, func=AF.Exp,
                                              bias=V0[:, 211:212], accum_out=V0[:, 212:213]))
            p.fence()
            vop(lambda e: e.reciprocal(V0[:, 213:214], V0[:, 212:213]))
            p.fence()
            vop(lambda e, lt=lt: e.tensor_scalar(out=lt, in0=lt, scalar1=V0[:, 213:214],
                                                 scalar2=None, op0=ALU.mult))
            p.fence()
        for t in range(16):
            for c in range(2):
                tr(A1[:, 8192 + t * 256 + c * 128: 8192 + t * 256 + (c + 1) * 128],
                   A1[:, 12288 + c * 2048 + t * 128: 12288 + c * 2048 + (t + 1) * 128])
        p.fence()
        cut("F0c")
        ld("y", A0[:, :], d_xn2tok, D)
        p.fence()
        for dc in range(8):
            last = None
            for t in range(16):
                last = top(lambda e, dc=dc, t=t: e.matmul(
                    bank(dc)[:, 0:256], A0[:, t * D + dc * 128: t * D + (dc + 1) * 128],
                    A2[:, t * 256:(t + 1) * 256],
                    start=(t == 0), stop=(t == 15)))
            vop(lambda e, dc=dc: e.tensor_copy(out=A4[:, dc * 256:(dc + 1) * 256],
                                               in_=bank(dc)[:, 0:256]), after=[last])
        p.fence()
        cut("F0d")
        for ex in range(16):
            lw1 = ld("y", A5[:, 0:2048], w1_in[ex], 256)
            lb1 = dma("y", A6[0:16, 0:256], b1_in[ex:ex + 1, :].partition_broadcast(16))
            last = None
            for dc in range(8):
                last = top(lambda e, ex=ex, dc=dc: e.matmul(
                    bank(0)[0:16, 0:256], A4[:, dc * 256 + ex * 16: dc * 256 + ex * 16 + 16],
                    A5[:, dc * 256:(dc + 1) * 256],
                    start=(dc == 0), stop=(dc == 7)), after=[lw1])
            hb = A6[0:16, 256:512]
            tk = vop(lambda e, hb=hb: e.tensor_add(hb, bank(0)[0:16, 0:256], A6[0:16, 0:256]),
                     after=[last, lb1])
            tk = sop(lambda e, hb=hb: e.activation(out=hb, in_=hb, func=AF.Gelu), after=[tk])
            cut("F0d1")
            trt = [tr(hb[:, c * 128:(c + 1) * 128], A6[:, 512 + c * 16: 512 + (c + 1) * 16],
                      after=[tk]) for c in range(2)]
            cut("F0d2")
            lw2 = ld("y", A5[:, 2048:4096], w2_in[ex], D)
            lb2 = dma("y", A6[0:16, 544:1568], b2_in[ex:ex + 1, :].partition_broadcast(16))
            p.fence()
            cut("F0d2b")
            so_toks = []
            prev_ev = None
            for db in range(2):
                last = None
                for hc in range(2):
                    last = top(lambda e, db=db, hc=hc: e.matmul(
                        bank(1)[0:16, :], A6[:, 512 + hc * 16: 512 + (hc + 1) * 16],
                        A5[:, 2048 + hc * 1024 + db * 512: 2048 + hc * 1024 + (db + 1) * 512],
                        start=(hc == 0), stop=(hc == 1)), after=[lw2, prev_ev] + trt)
                if db == 0:
                    cut("F0d2c")
                prev_ev = vop(lambda e, db=db: e.tensor_add(
                    A6[0:16, 1568 + db * 512: 1568 + (db + 1) * 512], bank(1)[0:16, :],
                    A6[0:16, 544 + db * 512: 544 + (db + 1) * 512]), after=[last, lb2])
                so_toks.append(prev_ev)
                if db == 0:
                    cut("F0d2d")
            cut("F0d3")
            dma("g", A4[16 * (ex % 8):16 * (ex % 8) + 16,
                        2048 + (ex // 8) * 1024: 2048 + (ex // 8 + 1) * 1024],
                A6[0:16, 1568:2592], after=so_toks)
            p.fence()
            cut("F0d4")
        cut("F0e")
        for tt in range(4):
            lx = ld("y", A2[:, :], d_xT[:, tt * 512:(tt + 1) * 512], 512)
            la = ld("y", A3[:, :], d_a0T[:, tt * 512:(tt + 1) * 512], 512)
            t2_toks = []
            for dc in range(8):
                last = None
                for ec in range(2):
                    last = top(lambda e, dc=dc, ec=ec, tt=tt: e.matmul(
                        bank(dc), A4[:, 2048 + ec * 1024 + dc * 128: 2048 + ec * 1024 + (dc + 1) * 128],
                        A1[:, 12288 + ec * 2048 + tt * 512: 12288 + ec * 2048 + (tt + 1) * 512],
                        start=(ec == 0), stop=(ec == 1)))
                t1tok = vop(lambda e, dc=dc: e.tensor_add(A5[:, dc * 512:(dc + 1) * 512],
                                                          bank(dc), A2[:, dc * 512:(dc + 1) * 512]),
                            after=[last, lx])
                t2_toks.append(vop(lambda e, dc=dc: e.tensor_add(A5[:, dc * 512:(dc + 1) * 512],
                                                                 A5[:, dc * 512:(dc + 1) * 512],
                                                                 A3[:, dc * 512:(dc + 1) * 512]),
                                   after=[t1tok, la]))
            st("g", d_x1T[:, tt * 512:(tt + 1) * 512], A5[:, 0:4096], 512, after=t2_toks)
            p.fence()
        if DEBUG:
            dma("g", dbg["dbg_x1T"], d_x1T)

        cut("F0")
        ld("y", A0[:, :], d_x1T, T)
        p.fence()
        for c in range(8):
            half = A5[:, 0:2048] if c % 2 == 0 else A5[:, 2048:4096]
            stok = sop(lambda e, c=c, half=half: e.activation(out=half, in_=A0[:, c * 2048:(c + 1) * 2048],
                                                              func=AF.Square))
            for tb in range(4):
                top(lambda e, c=c, tb=tb: e.matmul(
                    bank(tb)[0:1, :], ONESCOL, A0[:, c * 2048 + tb * 512: c * 2048 + (tb + 1) * 512],
                    start=(c == 0), stop=(c == 7)))
                top(lambda e, c=c, tb=tb, half=half: e.matmul(
                    bank(4 + tb)[0:1, :], ONESCOL, half[:, tb * 512:(tb + 1) * 512],
                    start=(c == 0), stop=(c == 7)), after=[stok])
            p.fence()
        for tb in range(4):
            vop(lambda e, tb=tb: e.tensor_copy(out=A2[0:1, tb * 512:(tb + 1) * 512], in_=bank(tb)[0:1, :]))
            vop(lambda e, tb=tb: e.tensor_copy(out=A2[0:1, 2048 + tb * 512: 2048 + (tb + 1) * 512],
                                               in_=bank(4 + tb)[0:1, :]))
        p.fence()
        vop(lambda e: e.tensor_scalar_mul(A2[0:1, 0:2048], A2[0:1, 0:2048], 1.0 / D))
        vop(lambda e: e.tensor_scalar_mul(A2[0:1, 2048:4096], A2[0:1, 2048:4096], 1.0 / D))
        p.fence()
        vop(lambda e: e.tensor_mul(A5[0:1, 0:2048], A2[0:1, 0:2048], A2[0:1, 0:2048]))
        p.fence()
        vop(lambda e: e.tensor_sub(A2[0:1, 2048:4096], A2[0:1, 2048:4096], A5[0:1, 0:2048]))
        p.fence()
        vop(lambda e: e.tensor_scalar_add(A2[0:1, 2048:4096], A2[0:1, 2048:4096], 1e-5))
        p.fence()
        sop(lambda e: e.activation(out=A2[0:1, 2048:4096], in_=A2[0:1, 2048:4096], func=AF.Sqrt))
        p.fence()
        vop(lambda e: e.reciprocal(A2[0:1, 2048:4096], A2[0:1, 2048:4096]))
        p.fence()
        dma("g", d_stat1[0:1, :], A2[0:1, 0:2048])
        dma("g", d_stat1[1:2, :], A2[0:1, 2048:4096])
        p.fence()
        dma("y", A3[:, 0:2048], d_stat1[0:1, :].partition_broadcast(128))
        dma("y", A3[:, 2048:4096], d_stat1[1:2, :].partition_broadcast(128))
        p.fence()
        for c in range(8):
            vop(lambda e, c=c: e.tensor_sub(A0[:, c * 2048:(c + 1) * 2048],
                                            A0[:, c * 2048:(c + 1) * 2048], A3[:, 0:2048]))
        p.fence()
        for c in range(8):
            vop(lambda e, c=c: e.tensor_mul(A0[:, c * 2048:(c + 1) * 2048],
                                            A0[:, c * 2048:(c + 1) * 2048], A3[:, 2048:4096]))
        p.fence()

        cut("LN1")
        dma("y", A5[0:8, 0:1024], gateW)
        p.fence()
        for dc in range(8):
            tr(A5[0:8, dc * 128:(dc + 1) * 128], A5[:, 1024 + dc * 8: 1024 + (dc + 1) * 8])
        p.fence()
        for tb in range(4):
            last = None
            for dc in range(8):
                last = top(lambda e, tb=tb, dc=dc: e.matmul(
                    bank(tb)[0:8, :], A5[:, 1024 + dc * 8: 1024 + (dc + 1) * 8],
                    A0[:, dc * 2048 + tb * 512: dc * 2048 + (tb + 1) * 512],
                    start=(dc == 0), stop=(dc == 7)))
            vop(lambda e, tb=tb: e.tensor_copy(out=A5[0:8, 2048 + tb * 512: 2048 + (tb + 1) * 512],
                                               in_=bank(tb)[0:8, :]), after=[last])
        p.fence()
        dma("g", d_ag_in, A5[0:8, 2048:4096])
        p.fence()
        p.op("g", lambda e: e.collective_compute(
            "AllGather", ALU.bypass, replica_groups=[list(range(8))],
            ins=[d_ag_in[:, :]], outs=[d_ag_out[:, :]]), cc=True)
        p.fence()
        dma("y", A5[0:64, 0:2048], d_ag_out[:, :])
        p.fence()
        for c2 in range(16):
            slot = tr_state["i"] % 2
            tr_state["i"] += 1
            sl = PB[:, 1024 + slot * 512: 1024 + slot * 512 + 128]
            tk = top(lambda e, c2=c2, sl=sl: e.transpose(
                sl[0:128, 0:64], A5[0:64, c2 * 128:(c2 + 1) * 128], IDENT[0:64, 0:64]),
                after=[tr_state["guard"][slot]])
            ev = vop(lambda e, c2=c2, sl=sl: e.tensor_copy(
                out=A4[:, 0:1024].rearrange("p (c1 c2x e) -> p c1 c2x e", c1=8, c2x=16)[:, :, c2, :],
                in_=sl[0:128, 0:64].rearrange("p (c1 e) -> p c1 e", e=8)), after=[tk])
            tr_state["guard"][slot] = ev
        p.fence()
        vop(lambda e: e.tensor_scalar_mul(A4[:, 0:1024], A4[:, 0:1024], -20.0))
        p.fence()
        TL = A4[:, 0:1024]
        EXB = A4[:, 1024:2048]
        MR = A4[:, 2048:2176]
        SR = A4[:, 2176:2304]
        tl3 = TL.rearrange("p (c e) -> p c e", e=8)
        ex3 = EXB.rearrange("p (c e) -> p c e", e=8)
        mr3 = MR.rearrange("p c -> p c ()")
        for it in range(20):
            vop(lambda e: e.reduce_max(mr3, tl3, axis=AX))
            p.fence()
            vop(lambda e: e.tensor_tensor(out=ex3, in0=tl3,
                                          in1=mr3.broadcast_to([128, 128, 8]), op=ALU.subtract))
            p.fence()
            sop(lambda e: e.activation(out=EXB, in_=EXB, func=AF.Exp))
            p.fence()
            vop(lambda e: e.reduce_sum(SR.rearrange("p c -> p c ()"), ex3, axis=AX))
            p.fence()
            sop(lambda e: e.activation(out=SR, in_=SR, func=AF.Ln))
            p.fence()
            vop(lambda e: e.tensor_add(MR, MR, SR))
            p.fence()
            vop(lambda e: e.tensor_tensor(out=tl3, in0=tl3,
                                          in1=mr3.broadcast_to([128, 128, 8]), op=ALU.subtract))
            p.fence()
            vop(lambda e: e.reduce_max(V0[:, 300:308].rearrange("p e -> p e ()"),
                                       TL.rearrange("p (c e) -> p e c", e=8), axis=AX))
            p.fence()
            tr(V0[:, 300:308], V0[0:8, 308:436])
            p.fence()
            vop(lambda e: e.reduce_max(V0[0:8, 436:437], V0[0:8, 308:436], axis=AX))
            p.fence()
            tr(V0[0:8, 436:437], V0[0:1, 437:445])
            p.fence()
            dma("g", d_rep[0:1, 0:8], V0[0:1, 437:445])
            p.fence()
            dma("y", V0[:, 448:456], d_rep[0:1, 0:8].partition_broadcast(128))
            p.fence()
            vop(lambda e: e.tensor_tensor(out=ex3, in0=tl3,
                                          in1=V0[:, 448:456].rearrange("p e -> p () e").broadcast_to([128, 128, 8]),
                                          op=ALU.subtract))
            p.fence()
            sop(lambda e: e.activation(out=EXB, in_=EXB, func=AF.Exp))
            p.fence()
            vop(lambda e: e.reduce_sum(V0[:, 456:464].rearrange("p e -> p e ()"),
                                       EXB.rearrange("p (c e) -> p e c", e=8), axis=AX))
            p.fence()
            top(lambda e: e.matmul(bank(0)[0:8, 0:1], V0[:, 456:464], ONESCOL, start=True, stop=True))
            p.fence()
            vop(lambda e: e.tensor_copy(out=V0[0:8, 464:465], in_=bank(0)[0:8, 0:1]))
            p.fence()
            sop(lambda e: e.activation(out=V0[0:8, 464:465], in_=V0[0:8, 464:465], func=AF.Ln))
            p.fence()
            vop(lambda e: e.tensor_add(V0[0:8, 465:466], V0[0:8, 436:437], V0[0:8, 464:465]))
            p.fence()
            tr(V0[0:8, 465:466], V0[0:1, 466:474])
            p.fence()
            dma("g", d_rep[1:2, 0:8], V0[0:1, 466:474])
            p.fence()
            dma("y", V0[:, 474:482], d_rep[1:2, 0:8].partition_broadcast(128))
            p.fence()
            vop(lambda e: e.tensor_tensor(out=tl3, in0=tl3,
                                          in1=V0[:, 474:482].rearrange("p e -> p () e").broadcast_to([128, 128, 8]),
                                          op=ALU.subtract))
            p.fence()
        PI = A4[:, 1024:2048]
        sop(lambda e: e.activation(out=PI, in_=TL, func=AF.Exp))
        p.fence()
        pi3 = PI.rearrange("p (c e) -> p c e", e=8)
        CAND = A5[:, 0:1024]
        MK1 = A5[:, 1024:2048]
        MK2 = A5[:, 2048:3072]
        TMPB = A5[:, 3072:4096]
        WB = A6[:, 0:1024]
        c3 = lambda buf: buf.rearrange("p (c e) -> p c e", e=8)
        idx3 = IDX8.rearrange("p e -> p () e").broadcast_to([128, 128, 8])
        idxm3 = IDXM8.rearrange("p e -> p () e").broadcast_to([128, 128, 8])
        mrb = mr3.broadcast_to([128, 128, 8])
        vop(lambda e: e.reduce_max(mr3, pi3, axis=AX))
        p.fence()
        vop(lambda e: e.tensor_tensor(out=c3(CAND), in0=pi3, in1=mrb, op=ALU.is_equal))
        p.fence()
        vop(lambda e: e.tensor_tensor(out=c3(CAND), in0=c3(CAND), in1=idxm3, op=ALU.mult))
        p.fence()
        vop(lambda e: e.tensor_scalar_add(CAND, CAND, 8.0))
        p.fence()
        vop(lambda e: e.tensor_reduce(mr3, c3(CAND), axis=AX, op=ALU.min))
        p.fence()
        vop(lambda e: e.tensor_tensor(out=c3(MK1), in0=idx3, in1=mrb, op=ALU.is_equal))
        p.fence()
        vop(lambda e: e.tensor_scalar_mul(TMPB, MK1, 1e30))
        p.fence()
        vop(lambda e: e.tensor_sub(TMPB, PI, TMPB))
        p.fence()
        vop(lambda e: e.reduce_max(mr3, c3(TMPB), axis=AX))
        p.fence()
        vop(lambda e: e.tensor_tensor(out=c3(CAND), in0=c3(TMPB), in1=mrb, op=ALU.is_equal))
        p.fence()
        vop(lambda e: e.tensor_tensor(out=c3(CAND), in0=c3(CAND), in1=idxm3, op=ALU.mult))
        p.fence()
        vop(lambda e: e.tensor_scalar_add(CAND, CAND, 8.0))
        p.fence()
        vop(lambda e: e.tensor_reduce(mr3, c3(CAND), axis=AX, op=ALU.min))
        p.fence()
        vop(lambda e: e.tensor_tensor(out=c3(MK2), in0=idx3, in1=mrb, op=ALU.is_equal))
        p.fence()
        vop(lambda e: e.tensor_tensor(out=c3(WB), in0=c3(MK1),
                                      in1=PI[:, 0::8].rearrange("p c -> p c ()").broadcast_to([128, 128, 8]),
                                      op=ALU.mult))
        p.fence()
        vop(lambda e: e.tensor_tensor(out=c3(TMPB), in0=c3(MK2),
                                      in1=PI[:, 1::8].rearrange("p c -> p c ()").broadcast_to([128, 128, 8]),
                                      op=ALU.mult))
        p.fence()
        vop(lambda e: e.tensor_add(WB, WB, TMPB))
        p.fence()
        vop(lambda e: e.tensor_tensor(
            out=A5[:, 0:1024].rearrange("p (cl e c1) -> p cl e c1", cl=16, e=8),
            in0=WB.rearrange("p (c1 cl e) -> p cl e c1", c1=8, e=8),
            in1=COH.rearrange("p c1 -> p () () c1").broadcast_to([128, 16, 8, 8]),
            op=ALU.mult))
        p.fence()
        vop(lambda e: e.tensor_reduce(
            A5[:, 1024:1152].rearrange("p ce -> p ce ()"),
            A5[:, 0:1024].rearrange("p (ce c1) -> p ce c1", c1=8),
            axis=AX, op=ALU.add))
        p.fence()
        tr(A5[:, 1024:1152], A5[:, 1152:1280])
        p.fence()
        dma("g", d_wtok, A5[:, 1152:1280])
        p.fence()
        if DEBUG:
            dma("g", dbg["dbg_pi"], PI)
            dma("g", dbg["dbg_wtok"], d_wtok)
            p.fence()

        cut("S1")
        for oc in range(16):
            half = A2[:, (oc % 2) * 1024:(oc % 2) * 1024 + 1024]
            lw = ld("y", half, wqkv[:, oc * 128:(oc + 1) * 128], 128)
            ev = []
            for tb in range(4):
                last = None
                for dc in range(8):
                    last = top(lambda e, half=half, tb=tb, dc=dc: e.matmul(
                        bank(tb), half[:, dc * 128:(dc + 1) * 128],
                        A0[:, dc * 2048 + tb * 512: dc * 2048 + (tb + 1) * 512],
                        start=(dc == 0), stop=(dc == 7)), after=[lw])
                ev.append(vop(lambda e, tb=tb: e.tensor_copy(out=A5[:, tb * 512:(tb + 1) * 512],
                                                             in_=bank(tb)), after=[last]))
            dma("g", d_qkT[oc * 128:(oc + 1) * 128, :], A5[:, 0:2048], after=ev)
            p.fence()
        for db in range(2):
            lv = ld("y", A3[:, :], wqkv[:, 2048 + db * 512: 2048 + (db + 1) * 512], 512)
            for tc in range(16):
                last = None
                for dc in range(8):
                    last = top(lambda e, tc=tc, dc=dc: e.matmul(
                        bank(tc % 8), A0[:, dc * 2048 + tc * 128: dc * 2048 + (tc + 1) * 128],
                        A3[:, dc * 512:(dc + 1) * 512],
                        start=(dc == 0), stop=(dc == 7)), after=[lv])
                tk = vop(lambda e, tc=tc: e.tensor_copy(out=A5[:, (tc % 8) * 512:(tc % 8) * 512 + 512],
                                                        in_=bank(tc % 8)), after=[last])
                dma("g", d_vtok[tc * 128:(tc + 1) * 128, db * 512:(db + 1) * 512],
                    A5[:, (tc % 8) * 512:(tc % 8) * 512 + 512], after=[tk])
                p.fence()
        for h in range(16):
            hp, hsel = h // 2, h % 2
            lk = dma("y", A2[0:64, 0:2048], d_qkT[1024 + h * 64: 1024 + (h + 1) * 64, :])
            lq = dma("y", A3[0:64, 0:2048], d_qkT[h * 64:(h + 1) * 64, :])
            lv = ld("y", A2[:, 2048:3072], d_vtok[:, h * 64:(h + 1) * 64], 64)
            prev_etok = None
            prev_tr_last = None
            prev_pv_evac = None
            for qt in range(16):
                nk = (qt + 1) * 128
                smm = []
                for kb in range((nk + 511) // 512):
                    n = min(512, nk - kb * 512)
                    smm.append(top(lambda e, qt=qt, kb=kb, n=n: e.matmul(
                        PS[:, kb * 512: kb * 512 + n], A3[0:64, qt * 128:(qt + 1) * 128],
                        A2[0:64, kb * 512: kb * 512 + n], start=True, stop=True),
                        after=[lk, lq, prev_etok]))
                mtok = vop(lambda e, nk=nk: e.tensor_add(PS[:, nk - 128:nk], PS[:, nk - 128:nk], CAUSAL),
                           after=smm)
                mtok = vop(lambda e, nk=nk: e.reduce_max(V0[:, 0:1], PS[:, 0:nk], axis=AX), after=[mtok])
                mtok = vop(lambda e: e.tensor_scalar_mul(V0[:, 1:2], V0[:, 0:1], -0.125), after=[mtok])
                etok = sop(lambda e, nk=nk: e.activation(
                    out=A5[:, 0:nk], in_=PS[:, 0:nk], func=AF.Exp, scale=0.125,
                    bias=V0[:, 1:2], accum_out=V0[:, 2:3]), after=[mtok, prev_tr_last])
                prev_etok = etok
                rtok = vop(lambda e: e.reciprocal(V0[:, 3:4], V0[:, 2:3]), after=[etok])
                ntok = vop(lambda e, nk=nk: e.tensor_scalar(
                    out=A5[:, 0:nk], in0=A5[:, 0:nk], scalar1=V0[:, 3:4], scalar2=None,
                    op0=ALU.mult), after=[rtok])
                pv_last = None
                tr_last = None
                for kc in range(qt + 1):
                    ttk = tr(A5[:, kc * 128:(kc + 1) * 128], A5[:, 2048 + kc * 128: 2048 + (kc + 1) * 128],
                             after=[ntok, prev_pv_evac])
                    tr_last = ttk
                    pv_last = top(lambda e, kc=kc, qt=qt, hsel=hsel: e.matmul(
                        PB[0:64, 0:128] if hsel == 0 else PB[64:128, 0:128],
                        A2[:, 2048 + kc * 64: 2048 + (kc + 1) * 64],
                        A5[:, 2048 + kc * 128: 2048 + (kc + 1) * 128],
                        start=(kc == 0), stop=(kc == qt),
                        tile_position=(0, 64 * hsel)), after=[ttk, lv, prev_pv_evac if kc == 0 else None])
                prev_tr_last = tr_last
                sl = PB[0:64, 0:128] if hsel == 0 else PB[64:128, 0:128]
                dstA6 = A6[0:64, qt * 128:(qt + 1) * 128] if hsel == 0 else A6[64:128, qt * 128:(qt + 1) * 128]
                prev_pv_evac = vop(lambda e, sl=sl, dstA6=dstA6: e.tensor_copy(out=dstA6, in_=sl), after=[pv_last])
            p.fence()
            if hsel == 1:
                dma("g", d_at1T[hp * 128:(hp + 1) * 128, :], A6[:, 0:2048])
                p.fence()
        if DEBUG:
            dma("g", dbg["dbg_at1T"], d_at1T)
        for tt in range(4):
            lat = ld("y", A2[:, :], d_at1T[:, tt * 512:(tt + 1) * 512], 512)
            lx1 = ld("y", A3[:, :], d_x1T[:, tt * 512:(tt + 1) * 512], 512)
            for dc in range(8):
                half = A6[:, (dc % 2) * 1024:(dc % 2) * 1024 + 1024]
                lw = ld("y", half, wout[:, dc * 128:(dc + 1) * 128], 128)
                last = None
                for hc in range(8):
                    last = top(lambda e, half=half, dc=dc, hc=hc: e.matmul(
                        bank(dc), half[:, hc * 128:(hc + 1) * 128],
                        A2[:, hc * 512:(hc + 1) * 512],
                        start=(hc == 0), stop=(hc == 7)), after=[lw, lat])
                vop(lambda e, dc=dc, tt=tt: e.tensor_add(
                    A1[:, dc * 2048 + tt * 512: dc * 2048 + (tt + 1) * 512],
                    bank(dc), A3[:, dc * 512:(dc + 1) * 512]), after=[last, lx1])
                p.fence()
        p.fence()

        cut("A1")
        bgF = [None] * 8
        wrep_last = None
        pcons = [None] * 4
        gscons = [None] * 4
        stg_g = [A2[:, 0:1024], A2[:, 2048:3072]]
        stg_u = [A2[:, 1024:2048], A2[:, 3072:4096]]
        stg_d = [A5[:, 2048:3072], A5[:, 3072:4096]]
        wgt = [A3[:, 0:1024], A3[:, 2048:3072]]
        wut = [A3[:, 1024:2048], A3[:, 3072:4096]]
        wdt = [A4[:, 2048:3072], A4[:, 3072:4096]]
        trg_last = [None, None]
        tru_last = [None, None]
        trd_last = [None, None]
        stg_g_last = [None, None]
        stg_u_last = [None, None]
        stg_d_last = [None, None]
        for ex in range(8):
            wr_toks = []
            for tt in range(4):
                wr_toks.append(dma(
                    "y", A4[:, tt * 512:(tt + 1) * 512].rearrange("p (r q) -> p r q", q=128),
                    d_wtok.rearrange("(cl e) p -> e cl p", e=8)[ex, tt * 4:(tt + 1) * 4, :][None]
                    .partition_broadcast(128), after=[wrep_last]))
            for fs in range(32):
                par = fs % 2
                lwg = dma("y", stg_g[par], Wgate[ex, fs * 128:(fs + 1) * 128, :],
                          after=[stg_g_last[par]])
                lwu = dma("y", stg_u[par], Wup[ex, fs * 128:(fs + 1) * 128, :],
                          after=[stg_u_last[par]])
                lwd = ld("y", stg_d[par], Wdown[ex, :, fs * 128:(fs + 1) * 128], 128,
                         after=[stg_d_last[par]])
                g_evs, u_evs, d_evs = [], [], []
                for dc in range(8):
                    e1 = tr(stg_g[par][:, dc * 128:(dc + 1) * 128],
                            wgt[par][:, dc * 128:(dc + 1) * 128],
                            after=[lwg, trg_last[par]], evac_eng="s")
                    e2 = tr(stg_u[par][:, dc * 128:(dc + 1) * 128],
                            wut[par][:, dc * 128:(dc + 1) * 128],
                            after=[lwu, tru_last[par]], evac_eng="s")
                    e3 = tr(stg_d[par][:, dc * 128:(dc + 1) * 128],
                            wdt[par][:, dc * 128:(dc + 1) * 128],
                            after=[lwd, trd_last[par]], evac_eng="s")
                    g_evs.append(e1); u_evs.append(e2); d_evs.append(e3)
                stg_g_last[par] = g_evs[-1]
                stg_u_last[par] = u_evs[-1]
                stg_d_last[par] = d_evs[-1]
                sg = [None] * 4
                glast = {}
                for dc in range(8):
                    for tb in range(4):
                        glast[tb] = top(lambda e, par=par, tb=tb, dc=dc: e.matmul(
                            bank(tb), wgt[par][:, dc * 128:(dc + 1) * 128],
                            A0[:, dc * 2048 + tb * 512: dc * 2048 + (tb + 1) * 512],
                            start=(dc == 0), stop=(dc == 7)),
                            after=([bgF[0], bgF[1], bgF[2], bgF[3]] + g_evs)
                            if dc == 0 and tb == 0 else [])
                for tb in range(4):
                    sg[tb] = sop(lambda e, tb=tb: e.activation(
                        out=A5[:, tb * 512:(tb + 1) * 512], in_=bank(tb), func=AF.Silu),
                        after=[glast[tb], gscons[tb]])
                    bgF[tb] = sg[tb]
                trg_last[par] = glast[3]
                ulast = {}
                ubmap = [4, 5, 0, 1]
                for dc in range(8):
                    for tb in range(4):
                        ub = ubmap[tb]
                        ulast[tb] = top(lambda e, par=par, tb=tb, dc=dc, ub=ub: e.matmul(
                            bank(ub), wut[par][:, dc * 128:(dc + 1) * 128],
                            A0[:, dc * 2048 + tb * 512: dc * 2048 + (tb + 1) * 512],
                            start=(dc == 0), stop=(dc == 7)),
                            after=([bgF[4], bgF[5], bgF[0], bgF[1]] + u_evs)
                            if dc == 0 and tb == 0 else [])
                for tb in range(4):
                    ub = ubmap[tb]
                    tk = vop(lambda e, tb=tb, ub=ub: e.tensor_mul(
                        A6[:, tb * 512:(tb + 1) * 512], bank(ub),
                        A5[:, tb * 512:(tb + 1) * 512]), after=[ulast[tb], sg[tb], pcons[tb]])
                    gscons[tb] = tk
                    tk2 = vop(lambda e, tb=tb: e.tensor_mul(
                        A6[:, tb * 512:(tb + 1) * 512], A6[:, tb * 512:(tb + 1) * 512],
                        A4[:, tb * 512:(tb + 1) * 512]), after=[tk] + (wr_toks if fs == 0 else []))
                    wrep_last = tk2
                    bgF[ub] = tk2
                tru_last[par] = ulast[3]
                dmm_last = None
                for dc in range(8):
                    for tb in range(4):
                        ub = ubmap[tb]
                        db_ = (dc * 4 + tb) % 6
                        aft = (list(bgF[0:6]) + d_evs) if (dc == 0 and tb == 0) else [bgF[db_]]
                        last = top(lambda e, par=par, tb=tb, dc=dc, db_=db_: e.matmul(
                            bank(db_), wdt[par][:, dc * 128:(dc + 1) * 128],
                            A6[:, tb * 512:(tb + 1) * 512],
                            start=True, stop=True), after=aft)
                        dmm_last = last
                        pcons[tb] = last
                        bgF[db_] = vop(lambda e, dc=dc, tb=tb, db_=db_: e.tensor_add(
                            A1[:, dc * 2048 + tb * 512: dc * 2048 + (tb + 1) * 512],
                            A1[:, dc * 2048 + tb * 512: dc * 2048 + (tb + 1) * 512],
                            bank(db_)), after=[last])
                trd_last[par] = dmm_last
        p.fence()

        for t in range(16):
            toks = [tr(A1[:, c * 2048 + t * 128: c * 2048 + (t + 1) * 128],
                       A2[:, c * 128:(c + 1) * 128]) for c in range(8)]
            dma("g", out_d[t * 128:(t + 1) * 128, :], A2[:, 0:1024], after=toks)
            p.fence()

    except _DoneEmit:
        pass

    with ExitStack() as es2:
        for ev in Prog.EVENTS:
            p.sems[ev] = es2.enter_context(nc.semaphore("sem_" + ev))
        with nc.Block() as block:
            @block.tensor
            def _(engine):
                p.replay("t", engine)

            @block.vector
            def _(engine):
                p.replay("v", engine)

            @block.scalar
            def _(engine):
                p.replay("s", engine)

            @block.sync
            def _(engine):
                p.replay("y", engine)

            @block.gpsimd
            def _(engine):
                p.replay("g", engine)
                p.final_waits("g", engine)
        es.close()
    return nc


def make_in_maps(inputs):
    cst = host_constants()
    base = {
        "Wq": inputs["L0_attn_Wq"], "Wk": inputs["L0_attn_Wk"], "Wv": inputs["L0_attn_Wv"],
        "bq": inputs["L0_attn_bq"].reshape(1, D), "bk": inputs["L0_attn_bk"].reshape(1, D),
        "bv": inputs["L0_attn_bv"].reshape(1, D),
        "gamma": inputs["L0_ffn_ln_gamma"].reshape(1, D), "beta": inputs["L0_ffn_ln_beta"].reshape(1, D),
        "mu": np.asarray(inputs["L0_ffn_mu"]).reshape(D, 256),
        "f0scale": inputs["L0_ffn_scale"].reshape(1, 1),
        "w1": inputs["L0_ffn_w1"], "b1": inputs["L0_ffn_b1"],
        "w2": inputs["L0_ffn_w2"], "b2": inputs["L0_ffn_b2"],
        "wqkv": inputs["L1_attn_w_qkv"], "wout": inputs["L1_attn_w_out"],
        "gateW": inputs["L1_ffn_gate_W"],
        "Wgate": inputs["L1_ffn_W_gate"], "Wup": inputs["L1_ffn_W_up"],
        "Wdown": inputs["L1_ffn_W_down"],
    }
    base = {k: np.ascontiguousarray(np.asarray(v, dtype=np.float32)) for k, v in base.items()}
    base.update(cst)
    maps = []
    x = np.asarray(inputs["L0_attn_x"], dtype=np.float32)
    for c in range(8):
        m = dict(base)
        m["x"] = np.ascontiguousarray(x[c])
        oh = np.zeros((1, 8), np.float32)
        oh[0, c] = 1.0
        m["coreoh"] = oh
        maps.append(m)
    return maps


def kernel(**inputs):
    nc = build_program()
    maps = make_in_maps(inputs)
    res = run_bass_kernel_spmd(nc, maps, core_ids=list(range(8)))
    return np.stack([np.asarray(res.results[c]["out"]) for c in range(8)], axis=0)
